# version 25
# baseline (speedup 1.0000x reference)
import contextlib
import numpy as np
import ml_dtypes
import concourse.bass as bass
import concourse.mybir as mybir
from concourse.bass_utils import run_bass_kernel_spmd

F32 = mybir.dt.float32
BF16 = mybir.dt.bfloat16
AF = mybir.ActivationFunctionType
ALU = mybir.AluOpType
AX = mybir.AxisListType
BF = ml_dtypes.bfloat16

B, T, D = 4, 4096, 2048
H, DK, NOPE, ROPE, VD = 16, 128, 64, 64, 128
KVR, QR = 512, 768
E, HF, HS = 8, 1408, 2816
MY, CH = 2048, 1024
NCH = MY // CH
SCALE = DK ** -0.5
EPS_RMS = float(np.finfo(np.float32).eps)
EPS_LN = 1e-5
KT_B = T // 128
DT = D // 128
RT_KV = KVR // 128
RT_Q = QR // 128
HT = HF // 128

_TAPS = {}
_PHASES = "ALL"

ENGINES = ("sync", "gpsimd", "scalar", "vector", "tensor")


class Sem:
    __slots__ = ("h", "v")

    def __init__(self, es, nc, name):
        self.h = es.enter_context(nc.semaphore(name))
        self.v = 0

    def inc(self, n=1):
        self.v += n
        return (self.h, self.v, n)

    def now(self):
        return (self.h, self.v, 0)


def mx(toks):
    return [t for t in toks if t is not None]


class Prog:
    def __init__(self):
        self.ops = {e: [] for e in ENGINES}
        self.hwm = {e: {} for e in ENGINES}
        self.n = 0

    def emit(self, eng, fn, waits=(), incs=()):
        real = []
        for tok in waits:
            if tok is None:
                continue
            h, v, _n = tok
            if v <= 0:
                continue
            key = id(h)
            if self.hwm[eng].get(key, -1) < v:
                self.hwm[eng][key] = v
                real.append((h, v))
        self.ops[eng].append((real, fn, [t for t in incs if t is not None]))
        self.n += 1

    def install(self, block):
        for eng in ENGINES:
            ops = self.ops[eng]
            if not ops:
                continue
            deco = getattr(block, eng)

            def make(ops=ops):
                def _(e):
                    for waits, fn, incs in ops:
                        for (h, v) in waits:
                            e.wait_ge(h, v)
                        bi = fn(e)
                        for (h, _v, n) in incs:
                            bi = bi.then_inc(h, n)
                return _
            deco(make())


class Builder:
    def __init__(self, taps):
        self.taps = dict(taps)
        nc = bass.Bass("TRN2", target_bir_lowering=False, debug=False)
        self.nc = nc
        self.es = contextlib.ExitStack()
        self.P = Prog()
        es = self.es
        dram = {}

        def din(name, shape, dt=F32):
            dram[name] = nc.dram_tensor(name, list(shape), dt, kind="ExternalInput").ap()
        din("xb", [T, D])
        din("xmy", [MY, D])
        din("wqa_p", [128, DT * QR])
        din("wqb_p", [128, RT_Q * D])
        din("qnorm_p", [128, RT_Q])
        din("wkva_p", [128, DT * 576])
        din("wkvb_p", [128, RT_KV * 3072])
        din("kvnorm_p", [128, RT_KV])
        din("wo_p", [4, 2, 128, 4096])
        din("gw_p", [128, DT * E])
        din("wgu_p", [E, HT, 2, 128, DT * 128])
        din("wd_p", [E, 4, HT * 128, 512])
        din("swgu_p", [2, HT, 2, 128, DT * 128])
        din("swd_p", [2, 4, HT * 128, 512])
        din("cosb", [128, KT_B * 32])
        din("sinb", [128, KT_B * 32])
        din("c16", [MY // 128, 128, 512])
        din("s16", [MY // 128, 128, 512])
        din("ident", [128, 128], BF16)
        din("onesrow", [1, 128], BF16)
        din("onesneg", [1, 128], BF16)
        din("onescol", [128, 1], BF16)
        self.dram = dram
        self.out = nc.dram_tensor("out", [MY, D], F32, kind="ExternalOutput").ap()
        self.x1s = nc.dram_tensor("x1s", [MY, D], F32, kind="Internal").ap()
        self.cnTs = nc.dram_tensor("cnTs", [128, RT_KV * T], BF16, kind="Internal").ap()
        self.kpeTs = nc.dram_tensor("kpeTs", [64, T], BF16, kind="Internal").ap()
        self.tap_out = {}
        for tname, (shape, dt) in self.taps.items():
            self.tap_out[tname] = nc.dram_tensor(tname, list(shape), dt,
                                                 kind="ExternalOutput").ap()

        def sb(name, shape, dt):
            return es.enter_context(nc.sbuf_tensor(name, list(shape), dt))
        arw = sb("arw", [128, 24576], BF16)
        arbig = sb("arbig", [128, 32768], BF16)
        regt = sb("regt", [128, 6144], F32)
        arwF = arw[:, :].bitcast(F32)
        arbF = arbig[:, :].bitcast(F32)
        regtB = regt[:, :].bitcast(BF16)
        self.wkva = arw[:, 0:DT * 576]
        self._cnb2b_view = arw[:, 17408:18176]
        self.arw_rms_scr = arwF[:, 9088:9600]
        self._xnbr0 = arw[:, 13312:15360]
        self._xnbr1 = arw[:, 15360:17408]
        self.cosb = arwF[:, 4608:4608 + KT_B * 32]
        self.sinb = arwF[:, 4608 + KT_B * 32:4608 + 2 * KT_B * 32]
        self.kpeT = arw[0:64, 20480:24576]
        self.wqa = arw[:, 0:DT * QR]
        self.wqb = arw[:, DT * QR:DT * QR + RT_Q * D]
        self.wkvb = arw[:, 0:RT_KV * 3072]
        self.ksT = arw[:, 12288:16384]
        self.vnat = arw[:, 16384:20480]
        self.woslab = [arw[:, 0:4096], arw[:, 4096:8192]]
        self.fin4 = [arwF[:, 4096 + i * 2048:4096 + (i + 1) * 2048] for i in range(4)]
        self.wgu = [arw[:, 0:4096], arw[:, 4096:8192]]
        self.wds = [arw[:, 8192:8192 + HT * 512], arw[:, 8192 + HT * 512:8192 + 2 * HT * 512]]
        self.qcatT = arbig[:, 0:H * CH]
        self.cnT = arbig[:, 16384:16384 + RT_KV * T]
        self.Y = arbF[:, 0:(CH // 128) * D]
        self.xst = [regt[:, 0:D], regt[:, D:2 * D]]
        self.c16s = regt[:, 4096:4608]
        self.s16s = regt[:, 4608:5120]
        self.ropetmp = regt[:, 5120:5632]
        self.accum2 = [regt[:, 4096:4608], regt[:, 4608:5120]]
        self._pt2_init_done = False
        self.Gs = regtB[:, 0:HT * CH]
        self.identb = sb("identb", [128, 128], BF16)
        self.onesrow = sb("onesrow_s", [1, 128], BF16)
        self.onesneg = sb("onesneg_s", [1, 128], BF16)
        self.onescol = sb("onescol_s", [128, 1], BF16)
        self.gw = sb("gw_s", [128, DT * E], BF16)
        self.nrmq = sb("nrmq_s", [128, RT_Q], F32)
        self.nrmkv = sb("nrmkv_s", [128, RT_KV], F32)
        self.stats = sb("stats", [128, 104], F32)
        self.macc = sb("macc", [128, 16], F32)
        self.maccb = sb("maccb", [128, 16], BF16)
        self.tmp8 = sb("tmp8", [128, 16], F32)
        self.e8 = sb("e8", [128, 8], F32)
        self.comb = sb("comb", [128, (CH // 128) * E], F32)
        self.combtmp = sb("combtmp", [128, 8], F32)
        self.mrow = sb("mrow", [1, 2 * CH], BF16)
        self.lrow = sb("lrow", [1, 512], F32)
        self.rrow = sb("rrow", [1, 512], F32)
        self.rrowb = sb("rrowb", [1, 512], BF16)
        self.sg = sb("sg", [128, CH], BF16)
        regtB2 = regtB
        self.pT2 = [[regtB2[:, 10240:10752], regtB2[:, 10752:11264], self.sg[:, 0:512]],
                    [regtB2[:, 11264:11776], regtB2[:, 11776:12288], self.sg[:, 512:1024]]]
        self.cnb = sb("cnb", [128, 768], BF16)
        scr = sb("scr", [128, 8192], BF16)
        self.xnb = scr[:, 0:2048]
        self.xnT = scr[:, 2048:4096]
        self.qf32 = scr[:, :].bitcast(F32)[:, 2048:4096]
        self.ksT2 = [self.ksT, scr[:, 0:4096]]
        self.vnat2 = [self.vnat, scr[:, 4096:8192]]
        self.x1st = sb("x1st", [128, D], F32)
        self.xn2T = sb("xn2T", [128, DT * CH], BF16)

        def ps(name, shape, dt):
            return es.enter_context(nc.psum_tensor(name, list(shape), dt))
        self.PS = [ps("psb%d" % i, [128, 512], F32) for i in range(4)]
        self.PS45 = ps("ps45", [128, 1024], F32)
        self.PS67 = ps("ps67", [128, 1024], F32)
        self.PS45b = self.PS45[:, :].bitcast(BF16)
        self.PS67b = self.PS67[:, :].bitcast(BF16)

        self.dS = Sem(es, nc, "dS")
        self.dG = Sem(es, nc, "dG")
        self.dO = Sem(es, nc, "dO")
        self.sPE = Sem(es, nc, "sPE")
        self.sVE = Sem(es, nc, "sVE")
        self.sSE = Sem(es, nc, "sSE")

    def mm(self, out, lhsT, rhs, start, stop, waits=(), incs=()):
        self.P.emit("tensor", (lambda e, o=out, l=lhsT, r=rhs, s=start, st=stop:
                               e.matmul(o, l, r, start=s, stop=st)), waits, incs)

    def tr(self, out, in_, waits=(), incs=()):
        self.P.emit("tensor", (lambda e, o=out, i=in_, idn=self.identb[:, :]:
                               e.transpose(o, i, idn)), waits, incs)

    def act(self, out, in_, func, bias=0.0, scale=1.0, accum=None, waits=(), incs=()):
        self.P.emit("scalar", (lambda e, o=out, i=in_, f=func, b=bias, s=scale, a=accum:
                               e.activation(o, i, f, bias=b, scale=s, accum_out=a)), waits, incs)

    def scopy(self, out, in_, waits=(), incs=()):
        self.P.emit("scalar", (lambda e, o=out, i=in_: e.copy(o, i)), waits, incs)

    def vcopy(self, out, in_, waits=(), incs=()):
        self.P.emit("vector", (lambda e, o=out, i=in_: e.tensor_copy(o, i)), waits, incs)

    def vtt(self, out, in0, in1, op, waits=(), incs=()):
        self.P.emit("vector", (lambda e, o=out, a=in0, b=in1, p=op:
                               e.tensor_tensor(o, a, b, p)), waits, incs)

    def vts(self, out, in0, s1, s2, op0, op1, waits=(), incs=()):
        self.P.emit("vector", (lambda e, o=out, a=in0, x=s1, y=s2, p0=op0, p1=op1:
                               e.tensor_scalar(o, a, x, y, p0, p1)), waits, incs)

    def vstt(self, out, in0, scalar, in1, op0, op1, waits=(), incs=()):
        self.P.emit("vector", (lambda e, o=out, a=in0, s=scalar, b=in1, p0=op0, p1=op1:
                               e.scalar_tensor_tensor(o, a, s, b, p0, p1)), waits, incs)

    def vred(self, out, in_, op, waits=(), incs=()):
        self.P.emit("vector", (lambda e, o=out, i=in_, p=op:
                               e.tensor_reduce(o, i, AX.X, p)), waits, incs)

    def vmemset(self, ap, val, waits=(), incs=()):
        self.P.emit("vector", (lambda e, a=ap, v=val: e.memset(a, v)), waits, incs)

    def vrecip(self, out, in_, waits=(), incs=()):
        self.P.emit("vector", (lambda e, o=out, i=in_: e.reciprocal(o, i)), waits, incs)

    def dma(self, queue, out, in_, sem, waits=()):
        tok = sem.inc(16)
        self.P.emit(queue, (lambda e, o=out, i=in_: e.dma_start(out=o, in_=i)), waits, [tok])
        return tok

    def tap(self, name, src_ap, waits):
        if name in self.tap_out:
            self.dma("sync", self.tap_out[name], src_ap, self.dO, waits)

    def barrier(self):
        return [self.sPE.now(), self.sVE.now(), self.sSE.now()]

    def ln_stats(self, srcs, n_feat, eps, base, waits, scratch=None):
        st = self.stats
        g = self.stats_guard.get(base)
        if scratch is None:
            scratch = self.qf32
            gq = self.t_qf32_free
        else:
            gq = None
        rstd = st[:, base:base + 1]
        bias = st[:, base + 1:base + 2]
        s_sum = st[:, base + 2:base + 3]
        s_sq = st[:, base + 3:base + 4]
        s_var = st[:, base + 4:base + 5]
        s_m = st[:, base + 5:base + 6]
        s_msq = st[:, base + 6:base + 7]
        vparts, sparts = [], []
        off = 0
        for j, (ap, w) in enumerate(srcs):
            pcol = st[:, base + 8 + j:base + 9 + j]
            tv = self.sVE.inc()
            self.vred(pcol, ap, ALU.add, waits=mx(list(waits) + [g]), incs=[tv])
            vparts.append((pcol, tv))
            psq = st[:, base + 12 + j:base + 13 + j]
            ts = self.sSE.inc()
            self.act(scratch[:, off:off + w], ap, AF.Square,
                     waits=mx(list(waits) + [g, gq]), incs=[ts])
            tvq = self.sVE.inc()
            self.vred(psq, scratch[:, off:off + w], ALU.add, waits=[ts], incs=[tvq])
            off += w
            sparts.append((psq, tvq))
        if scratch is self.qf32:
            self.t_qf32_free = sparts[-1][1]
        c1 = self.sVE.inc()
        if len(srcs) == 1:
            self.vcopy(s_sum, vparts[0][0], waits=[vparts[0][1]], incs=[c1])
        else:
            self.vtt(s_sum, vparts[0][0], vparts[1][0], ALU.add,
                     waits=[t for _, t in vparts], incs=([c1] if len(srcs) == 2 else []))
            for i, (pcol, _t) in enumerate(vparts[2:]):
                last = (i == len(vparts) - 3)
                self.vtt(s_sum, s_sum, pcol, ALU.add, waits=[self.sVE.now()],
                         incs=([c1] if last else []))
        c2 = self.sVE.inc()
        if len(srcs) == 1:
            self.vcopy(s_sq, sparts[0][0], waits=[sparts[0][1]], incs=[c2])
        else:
            self.vtt(s_sq, sparts[0][0], sparts[1][0], ALU.add,
                     waits=[t for _, t in sparts], incs=([c2] if len(srcs) == 2 else []))
            for i, (psq, _t) in enumerate(sparts[2:]):
                last = (i == len(sparts) - 3)
                self.vtt(s_sq, s_sq, psq, ALU.add, waits=[self.sVE.now()],
                         incs=([c2] if last else []))
        m1 = self.sVE.inc()
        self.vts(s_m, s_sum, 1.0 / n_feat, None, ALU.mult, ALU.bypass, waits=[c1], incs=[m1])
        v1 = self.sVE.inc()
        self.vts(s_var, s_sq, 1.0 / n_feat, float(eps), ALU.mult, ALU.add, waits=[c2],
                 incs=[v1])
        mm2 = self.sVE.inc()
        self.vtt(s_msq, s_m, s_m, ALU.mult, waits=[m1], incs=[mm2])
        v2 = self.sVE.inc()
        self.vtt(s_var, s_var, s_msq, ALU.subtract, waits=[v1, mm2], incs=[v2])
        ts1 = self.sSE.inc()
        self.act(st[:, base + 7:base + 8], s_var, AF.Ln, waits=[v2], incs=[ts1])
        ts2 = self.sSE.inc()
        self.act(rstd, st[:, base + 7:base + 8], AF.Exp, scale=-0.5, waits=[ts1], incs=[ts2])
        tv3 = self.sVE.inc()
        self.vstt(bias, s_m, -1.0, rstd, ALU.mult, ALU.mult, waits=[ts2, m1], incs=[tv3])
        return rstd, bias, ts2, tv3

    def rms_rstd(self, sq_parts, n_feat, eps, base):
        st = self.stats
        g = self.stats_guard.get(base)
        rstd = st[:, base:base + 1]
        s_sq = st[:, base + 1:base + 2]
        if len(sq_parts) == 1:
            tv = self.sVE.inc()
            self.vts(s_sq, sq_parts[0][0], 1.0 / n_feat, float(eps), ALU.mult, ALU.add,
                     waits=mx([sq_parts[0][1], g]), incs=[tv])
        else:
            c1 = self.sVE.inc()
            self.vtt(s_sq, sq_parts[0][0], sq_parts[1][0], ALU.add,
                     waits=mx([t for _, t in sq_parts] + [g]), incs=[c1])
            tv = self.sVE.inc()
            self.vts(s_sq, s_sq, 1.0 / n_feat, float(eps), ALU.mult, ALU.add, waits=[c1],
                     incs=[tv])
        ts1 = self.sSE.inc()
        self.act(st[:, base + 2:base + 3], s_sq, AF.Ln, waits=[tv], incs=[ts1])
        ts2 = self.sSE.inc()
        self.act(rstd, st[:, base + 2:base + 3], AF.Exp, scale=-0.5, waits=[ts1], incs=[ts2])
        return rstd, ts2

    def xn_tile(self, src_dram, row0, buf_i, base):
        base = base + 32 * buf_i
        xst = self.xst[buf_i]
        t_x = self.dma("sync", xst[:, :], src_dram[row0:row0 + 128, :], self.dS,
                       waits=mx([self.xst_free[buf_i]]))
        rstd, bias, ts, tv = self.ln_stats([(xst[:, :], D)], D, EPS_LN, base, [t_x])
        t_xn = self.sSE.inc()
        self.act(self.xnb[:, :], xst[:, :], AF.Identity, bias=bias, scale=rstd,
                 waits=mx([ts, tv, self.t_xnb_free]), incs=[t_xn])
        self.stats_guard[base] = t_xn
        self.xst_free[buf_i] = t_xn
        ev_toks = []
        for g in range(4):
            pt = [self.PS45b[:, 0:1024], self.PS45b[:, 1024:2048]][g % 2]
            prev_ev = self.pt_free[g % 2]
            for j in range(4):
                dt = g * 4 + j
                w = mx(([t_xn, self.t_ident] if dt == 0 else []) + ([prev_ev] if j == 0 else []))
                inc = [self.sPE.inc()] if j == 3 else []
                self.tr(pt[:, j * 128:(j + 1) * 128], self.xnb[:, dt * 128:(dt + 1) * 128],
                        waits=w, incs=inc)
            t_tr = self.sPE.now()
            t_ev = self.sVE.inc()
            self.vcopy(self.xnT[:, g * 512:(g + 1) * 512], pt[:, 0:512],
                       waits=mx([t_tr, self.t_xnT_free]), incs=[t_ev])
            self.pt_free[g % 2] = t_ev
            ev_toks.append(t_ev)
        self.t_xnb_free = self.sPE.now()
        return ev_toks, t_xn

    def p0(self):
        d = self.dram
        self.t_ident = self.dma("sync", self.identb[:, :], d["ident"], self.dS)
        self.dma("sync", self.onesrow[:, :], d["onesrow"], self.dS)
        self.dma("sync", self.onesneg[:, :], d["onesneg"], self.dS)
        self.t_ones = self.dma("sync", self.onescol[:, :], d["onescol"], self.dS)
        self.dma("sync", self.cosb, d["cosb"], self.dS)
        self.t_rope = self.dma("sync", self.sinb, d["sinb"], self.dS)
        self.t_wkva = self.dma("gpsimd", self.wkva, d["wkva_p"], self.dG)
        self.t_gw = self.dma("gpsimd", self.gw[:, :], d["gw_p"], self.dG)
        self.t_nq = self.dma("sync", self.nrmq[:, :], d["qnorm_p"], self.dS)
        self.t_nkv = self.dma("sync", self.nrmkv[:, :], d["kvnorm_p"], self.dS)

    def ln_stage_a(self, srcs, base, waits, scratch):
        st = self.stats
        g = self.stats_guard.get(base)
        vparts, sparts = [], []
        off = 0
        for j, (ap, w) in enumerate(srcs):
            pcol = st[:, base + 8 + j:base + 9 + j]
            tv = self.sVE.inc()
            self.vred(pcol, ap, ALU.add, waits=mx(list(waits) + [g]), incs=[tv])
            vparts.append((pcol, tv))
            psq = st[:, base + 12 + j:base + 13 + j]
            ts = self.sSE.inc()
            self.act(scratch[:, off:off + w], ap, AF.Square,
                     waits=mx(list(waits) + [g]), incs=[ts])
            tvq = self.sVE.inc()
            self.vred(psq, scratch[:, off:off + w], ALU.add, waits=[ts], incs=[tvq])
            off += w
            sparts.append((psq, tvq))
        return vparts, sparts

    def ln_stage_b(self, vparts, sparts, n_feat, eps, base):
        st = self.stats
        rstd = st[:, base:base + 1]
        bias = st[:, base + 1:base + 2]
        s_sum = st[:, base + 2:base + 3]
        s_sq = st[:, base + 3:base + 4]
        s_var = st[:, base + 4:base + 5]
        s_m = st[:, base + 5:base + 6]
        s_msq = st[:, base + 6:base + 7]
        c1 = self.sVE.inc()
        if len(vparts) == 1:
            self.vcopy(s_sum, vparts[0][0], waits=[vparts[0][1]], incs=[c1])
        else:
            self.vtt(s_sum, vparts[0][0], vparts[1][0], ALU.add,
                     waits=[t for _, t in vparts], incs=[c1])
        c2 = self.sVE.inc()
        if len(sparts) == 1:
            self.vcopy(s_sq, sparts[0][0], waits=[sparts[0][1]], incs=[c2])
        else:
            self.vtt(s_sq, sparts[0][0], sparts[1][0], ALU.add,
                     waits=[t for _, t in sparts], incs=[c2])
        m1 = self.sVE.inc()
        self.vts(s_m, s_sum, 1.0 / n_feat, None, ALU.mult, ALU.bypass, waits=[c1], incs=[m1])
        v1 = self.sVE.inc()
        self.vts(s_var, s_sq, 1.0 / n_feat, float(eps), ALU.mult, ALU.add, waits=[c2],
                 incs=[v1])
        mm2 = self.sVE.inc()
        self.vtt(s_msq, s_m, s_m, ALU.mult, waits=[m1], incs=[mm2])
        v2 = self.sVE.inc()
        self.vtt(s_var, s_var, s_msq, ALU.subtract, waits=[v1, mm2], incs=[v2])
        ts1 = self.sSE.inc()
        self.act(st[:, base + 7:base + 8], s_var, AF.Ln, waits=[v2], incs=[ts1])
        ts2 = self.sSE.inc()
        self.act(rstd, st[:, base + 7:base + 8], AF.Exp, scale=-0.5, waits=[ts1], incs=[ts2])
        tv3 = self.sVE.inc()
        self.vstt(bias, s_m, -1.0, rstd, ALU.mult, ALU.mult, waits=[ts2, m1], incs=[tv3])
        return rstd, bias, ts2, tv3

    def p1(self):
        arb = self.qcatT
        arbF = arb.bitcast(F32)
        ring = [arbF[:, i * 2048:(i + 1) * 2048] for i in range(4)]
        xnbr = [self._xnbr0, self._xnbr1]
        xnTr = [self.xnb, self.xnT]
        ring_free = [None, None, None, None]
        xnb_free = [None, None]
        xnT_free = [None, None]
        t_pc_free2 = [None, None]
        pc_banks = [(self.PS[0], self.PS[1]), (self.PS[2], self.PS[3])]
        stats_base3 = [0, 32, 16, 48]
        rms_base2 = [(88, 90), (92, 94)]
        stA = {}
        Q = {"b": [], "tr": [], "rms": [], "cn": [], "ctr": []}

        def do_dma(kt):
            i3 = kt % 4
            t_x = self.dma("sync", ring[i3], self.dram["xb"][kt * 128:(kt + 1) * 128, :],
                           self.dS, waits=mx([ring_free[i3]]))
            stA[kt] = t_x

        def do_stageA(kt):
            i3 = kt % 4
            scr = [self.qf32, self.x1st][kt % 2]
            vp, sp = self.ln_stage_a([(ring[i3], D)], stats_base3[i3], [stA[kt]],
                                     scratch=scr)
            stA[kt] = (vp, sp)

        def do_stageB_affine(kt):
            i3, i2 = kt % 4, kt % 2
            vp, sp = stA.pop(kt)
            rstd, bias, ts, tv = self.ln_stage_b(vp, sp, D, EPS_LN, stats_base3[i3])
            t_xn = self.sSE.inc()
            self.act(xnbr[i2][:, :], ring[i3], AF.Identity, bias=bias, scale=rstd,
                     waits=mx([ts, tv, xnb_free[i2]]), incs=[t_xn])
            self.stats_guard[stats_base3[i3]] = t_xn
            ring_free[i3] = t_xn
            Q["tr"].append((kt, t_xn))

        def do_tr_c(kt, t_xn):
            i2 = kt % 2
            ev_toks = []
            for g in range(4):
                pt = [self.PS45b[:, 0:1024], self.PS45b[:, 1024:2048]][g % 2]
                prev_ev = self.pt_free[g % 2]
                for j in range(4):
                    dt = g * 4 + j
                    w = mx(([t_xn, self.t_ident] if dt == 0 else []) +
                           ([prev_ev] if j == 0 else []))
                    inc = [self.sPE.inc()] if j == 3 else []
                    self.tr(pt[:, j * 128:(j + 1) * 128], xnbr[i2][:, dt * 128:(dt + 1) * 128],
                            waits=w, incs=inc)
                t_tr = self.sPE.now()
                t_ev = self.sVE.inc()
                self.vcopy(xnTr[i2][:, g * 512:(g + 1) * 512], pt[:, 0:512],
                           waits=mx([t_tr, xnT_free[i2]]), incs=[t_ev])
                self.pt_free[g % 2] = t_ev
                ev_toks.append(t_ev)
            xnb_free[i2] = self.sPE.now()
            pc0, pc1 = pc_banks[i2]
            for dt in range(DT):
                w = mx([ev_toks[dt // 4], self.t_wkva] +
                       ((t_pc_free2[i2] or []) if dt == 0 else []))
                self.mm(pc0[:, 0:512], xnTr[i2][:, dt * 128:(dt + 1) * 128],
                        self.wkva[:, dt * 576:dt * 576 + 512],
                        start=(dt == 0), stop=(dt == DT - 1), waits=w)
                inc = [self.sPE.inc()] if dt == DT - 1 else []
                self.mm(pc1[:, 0:64], xnTr[i2][:, dt * 128:(dt + 1) * 128],
                        self.wkva[:, dt * 576 + 512:(dt + 1) * 576],
                        start=(dt == 0), stop=(dt == DT - 1), incs=inc)
            t_c = self.sPE.now()
            xnT_free[i2] = t_c
            Q["rms"].append((kt, t_c))

        rms_scr = self.arw_rms_scr

        def do_rms_a(kt, t_c):
            i2 = kt % 2
            rb0, _rb1 = rms_base2[i2]
            pc0 = pc_banks[i2][0]
            ts_sq = self.sSE.inc()
            self.act(rms_scr, pc0[:, 0:512], AF.Square,
                     waits=mx([t_c, self.t_rms_scr]), incs=[ts_sq])
            tv_sq = self.sVE.inc()
            self.vred(self.stats[:, rb0:rb0 + 1], rms_scr, ALU.add,
                      waits=mx([ts_sq, self.stats_guard.get(rb0)]), incs=[tv_sq])
            self.stats_guard[rb0] = tv_sq
            self.t_rms_scr = tv_sq
            Q["cn"].append((kt, t_c, tv_sq))

        def do_cn(kt, t_c, tv_sq):
            i2 = kt % 2
            rb0, rb1 = rms_base2[i2]
            pc0, pc1 = pc_banks[i2]
            rstd, t_rstd = self.rms_rstd([(self.stats[:, rb0:rb0 + 1], tv_sq)], KVR,
                                         EPS_RMS, rb1)
            cnb = self.cnb[:, 0:768] if i2 == 0 else self._cnb2b_view
            t_cn = self.sSE.inc()
            self.act(cnb[:, 0:512], pc0[:, 0:512], AF.Copy, scale=rstd,
                     waits=mx([t_rstd, self.t_cnb_free2[i2]]), incs=[t_cn])
            self.stats_guard[rb1] = t_cn
            cos = self.cosb[:, kt * 32:(kt + 1) * 32]
            sin = self.sinb[:, kt * 32:(kt + 1) * 32]
            x1, x2 = pc1[:, 0:32], pc1[:, 32:64]
            rtp = self.ropetmp[:, 0:64] if i2 == 0 else self.ropetmp[:, 64:128]
            r0, r1 = rtp[:, 0:32], rtp[:, 32:64]
            e1 = self.sVE.inc()
            self.vtt(r0, x1, cos, ALU.mult, waits=mx([self.t_rope, self.t_cnb_free2[i2]]))
            self.vtt(r1, x2, sin, ALU.mult, incs=[e1])
            e2 = self.sVE.inc()
            self.vtt(cnb[:, 512:544], r0, r1, ALU.subtract, waits=[e1], incs=[e2])
            self.vtt(r0, x1, sin, ALU.mult, waits=[e2])
            t_kpe_r = self.sVE.inc()
            self.vtt(r1, x2, cos, ALU.mult, incs=[t_kpe_r])
            t_kpe = self.sVE.inc()
            self.vtt(cnb[:, 544:576], r0, r1, ALU.add, waits=[t_kpe_r], incs=[t_kpe])
            t_pc_free2[i2] = [t_kpe_r, t_cn]
            Q["ctr"].append((kt, t_cn, t_kpe, cnb))

        def do_cn_tr(kt, t_cn, t_kpe, cnb):
            i2 = kt % 2
            ptc = [self.PS67b[:, 0:1024], self.PS67b[:, 1024:2048]][i2]
            prev = self.pt2_free[i2]
            for rt in range(4):
                w = mx([t_cn] + ([prev] if rt == 0 else []))
                self.tr(ptc[:, rt * 128:(rt + 1) * 128],
                        cnb[:, rt * 128:(rt + 1) * 128], waits=w)
            t_trk = self.sPE.inc()
            self.tr(ptc[:, 512:640], cnb[:, 512:640], waits=mx([t_kpe]), incs=[t_trk])
            t_evc = self.sVE.inc()
            dst = self.cnT.rearrange("p (r t) -> p r t",
                                     r=RT_KV)[:, :, kt * 128:(kt + 1) * 128]
            srcv = ptc[:, 0:512].rearrange("p (r t) -> p r t", r=4)
            self.vcopy(dst, srcv, waits=[t_trk], incs=[t_evc])
            t_evk = self.sVE.inc()
            self.vcopy(self.kpeT[0:64, kt * 128:(kt + 1) * 128], ptc[0:64, 512:640],
                       waits=[t_trk], incs=[t_evk])
            self.pt2_free[i2] = t_evk
            self.t_cnb_free2[i2] = self.sPE.now()

        do_dma(0)
        do_dma(1)
        do_dma(2)
        for kt in range(KT_B):
            if kt >= 1:
                do_stageB_affine(kt - 1)
            do_stageA(kt)
            if kt + 3 < KT_B:
                do_dma(kt + 3)
            if len(Q["tr"]) >= 2:
                do_tr_c(*Q["tr"].pop(0))
            if Q["rms"]:
                do_rms_a(*Q["rms"].pop(0))
            if Q["cn"]:
                do_cn(*Q["cn"].pop(0))
            if len(Q["ctr"]) >= 2:
                do_cn_tr(*Q["ctr"].pop(0))
        do_stageB_affine(KT_B - 1)
        for nm in ("tr", "rms", "cn"):
            while Q[nm]:
                if nm == "tr":
                    do_tr_c(*Q["tr"].pop(0))
                    if Q["rms"]:
                        do_rms_a(*Q["rms"].pop(0))
                    if Q["cn"]:
                        do_cn(*Q["cn"].pop(0))
                elif nm == "rms":
                    do_rms_a(*Q["rms"].pop(0))
                    if Q["cn"]:
                        do_cn(*Q["cn"].pop(0))
                else:
                    do_cn(*Q["cn"].pop(0))
        while Q["ctr"]:
            do_cn_tr(*Q["ctr"].pop(0))
        self.t_cnT_done = self.sVE.now()
        self.t_cnT_spill = self.dma("sync", self.cnTs, self.cnT, self.dO,
                                    waits=[self.t_cnT_done])
        self.t_kpe_spill = self.dma("sync", self.kpeTs, self.kpeT, self.dO,
                                    waits=[self.t_cnT_done])
        self.tap("tap_cnT", self.cnT, [self.t_cnT_done])
        self.tap("tap_kpeT", self.kpeT, [self.t_cnT_done])

    def p2(self, ch, bar):
        base = ch * CH
        NT = CH // 128
        d = self.dram
        self.pt_free = [bar[1], bar[1]]
        self.pt2_free = [bar[1], bar[1]]
        self.t_xnb_free = bar[0]
        self.t_xnT_free = bar[0]
        self.t_qf32_free = bar[0]
        wq_w = (mx([self.t_kpe_spill, self.t_wgu_free[0], self.t_wgu_free[1],
                    self.t_wds_free[0], self.t_wds_free[1], bar[1], bar[2]])
                if ch > 0 else mx(bar + [self.t_kpe_spill]))
        t_wqa = self.dma("gpsimd", self.wqa, d["wqa_p"], self.dG, waits=wq_w)
        t_wqb = self.dma("gpsimd", self.wqb, d["wqb_p"], self.dG)
        t_wqb_f = None
        for rt in range(RT_Q):
            t_wqb_f = self.sVE.inc()
            self.vts(self.wqb[:, rt * D:(rt + 1) * D], self.wqb[:, rt * D:(rt + 1) * D],
                     self.nrmq[:, rt:rt + 1], None, ALU.mult, ALU.bypass,
                     waits=[t_wqb, self.t_nq], incs=[t_wqb_f])
        for tt in range(NT):
            ev_toks, _ = self.xn_tile(self.dram["xmy"], base + tt * 128, tt % 2, 0)
            pqa0, pqa1 = self.PS[0], self.PS[1]
            for dt in range(DT):
                w = mx([ev_toks[dt // 4], t_wqa] + ([self.t_pc_free] if dt == 0 else []))
                self.mm(pqa0[:, 0:512], self.xnT[:, dt * 128:(dt + 1) * 128],
                        self.wqa[:, dt * QR:dt * QR + 512],
                        start=(dt == 0), stop=(dt == DT - 1), waits=w)
                inc = [self.sPE.inc()] if dt == DT - 1 else []
                self.mm(pqa1[:, 0:256], self.xnT[:, dt * 128:(dt + 1) * 128],
                        self.wqa[:, dt * QR + 512:(dt + 1) * QR],
                        start=(dt == 0), stop=(dt == DT - 1), incs=inc)
            t_qa = self.sPE.now()
            ts0 = self.sSE.inc()
            self.act(self.qf32[:, 0:512], pqa0[:, 0:512], AF.Square,
                     waits=mx([t_qa, self.stats_guard.get(80), self.t_qf32_free]), incs=[ts0])
            self.t_qf32_free = None
            ts1 = self.sSE.inc()
            self.act(self.qf32[:, 512:768], pqa1[:, 0:256], AF.Square, incs=[ts1])
            tvs0 = self.sVE.inc()
            self.vred(self.stats[:, 80:81], self.qf32[:, 0:512], ALU.add,
                      waits=mx([ts0, self.stats_guard.get(80)]), incs=[tvs0])
            tvs1 = self.sVE.inc()
            self.vred(self.stats[:, 81:82], self.qf32[:, 512:768], ALU.add,
                      waits=[ts1], incs=[tvs1])
            self.stats_guard[80] = tvs1
            rstd, t_rstd = self.rms_rstd([(self.stats[:, 80:81], tvs0),
                                          (self.stats[:, 81:82], tvs1)], QR, EPS_RMS, 84)
            t_qan0 = self.sSE.inc()
            self.act(self.cnb[:, 0:512], pqa0[:, 0:512], AF.Copy, scale=rstd,
                     waits=mx([t_rstd, self.t_cnb_free]), incs=[t_qan0])
            t_qan = self.sSE.inc()
            self.act(self.cnb[:, 512:768], pqa1[:, 0:256], AF.Copy, scale=rstd, incs=[t_qan])
            self.stats_guard[84] = t_qan
            self.t_pc_free = t_qan
            prev = self.pt2_free[0]
            ptq = self.PS67b[:, 0:2048]
            for j in range(RT_Q):
                w = mx([t_qan0 if j < 4 else t_qan] + ([prev, self.pt2_free[1], t_qa]
                                                      if j == 0 else []))
                inc = [self.sPE.inc()] if j == RT_Q - 1 else []
                self.tr(ptq[:, j * 128:(j + 1) * 128],
                        self.cnb[:, j * 128:(j + 1) * 128], waits=w, incs=inc)
            t_trq = self.sPE.now()
            self.t_cnb_free = t_trq
            t_evq = self.sVE.inc()
            self.vcopy(self.xnT[:, 0:768], ptq[:, 0:768], waits=[t_trq], incs=[t_evq])
            self.pt2_free[0] = t_evq
            self.pt2_free[1] = t_evq
            ev_q = []
            for nchk in range(4):
                pq = self.PS[2 + (nchk % 2)]
                for rt in range(RT_Q):
                    w = mx([t_evq, t_wqb_f] +
                           ([ev_q[nchk - 2]] if (rt == 0 and nchk >= 2) else []) +
                           ([self.t_pq_free[nchk % 2]] if (rt == 0 and nchk < 2) else []))
                    inc = [self.sPE.inc()] if rt == RT_Q - 1 else []
                    self.mm(pq[:, 0:512], self.xnT[:, rt * 128:(rt + 1) * 128],
                            self.wqb[:, rt * D + nchk * 512:rt * D + (nchk + 1) * 512],
                            start=(rt == 0), stop=(rt == RT_Q - 1), waits=w, incs=inc)
                t_q = self.sPE.now()
                tev = self.sSE.inc()
                self.act(self.qf32[:, nchk * 512:(nchk + 1) * 512], pq[:, 0:512], AF.Copy,
                         waits=mx([t_q]), incs=[tev])
                ev_q.append(tev)
            t_qf = ev_q[-1]
            self.t_xnT_free = self.sPE.now()
            self.t_pq_free = [ev_q[2], ev_q[3]]
            t_c16 = self.dma("sync", self.c16s, self.dram["c16"][(base // 128) + tt, :, :],
                             self.dS, waits=mx([self.t_c16_free]))
            t_s16 = self.dma("sync", self.s16s, self.dram["s16"][(base // 128) + tt, :, :],
                             self.dS)
            t_qb = self.sSE.inc()
            self.scopy(self.xnb[:, :], self.qf32[:, :], waits=mx([t_qf, self.t_xnb_free]),
                       incs=[t_qb])

            def pe_ap(tn, off):
                return tn.rearrange("p (h c) -> p h c", h=H)[:, :, 64 + off:96 + off]
            x1, x2 = pe_ap(self.qf32[:, :], 0), pe_ap(self.qf32[:, :], 32)
            c16 = self.c16s.rearrange("p (h c) -> p h c", h=H)
            s16 = self.s16s.rearrange("p (h c) -> p h c", h=H)
            rtm = self.ropetmp.rearrange("p (h c) -> p h c", h=H)
            trs = self.x1st[:, 0:512].rearrange("p (h c) -> p h c", h=H)
            o1, o2 = pe_ap(self.xnb[:, :], 0), pe_ap(self.xnb[:, :], 32)
            f1 = self.sVE.inc()
            self.vtt(rtm, x1, c16, ALU.mult,
                     waits=mx([t_qf, t_c16, t_s16, self.t_ropetmp_free, self.t_x1st_free]))
            self.vtt(trs, x2, s16, ALU.mult, incs=[f1])
            f2 = self.sVE.inc()
            self.vtt(o1, rtm, trs, ALU.subtract, waits=[t_qb, f1], incs=[f2])
            self.vtt(rtm, x1, s16, ALU.mult, waits=[f2])
            f3 = self.sVE.inc()
            self.vtt(trs, x2, c16, ALU.mult, incs=[f3])
            tv_rope = self.sVE.inc()
            self.vtt(o2, rtm, trs, ALU.add, waits=[f3], incs=[tv_rope])
            self.t_c16_free = tv_rope
            self.t_qf32_free = tv_rope
            self.t_ropetmp_free = tv_rope
            self.t_x1st_free = tv_rope
            for g in range(4):
                pt = [self.PS45b[:, 0:1024], self.PS45b[:, 1024:2048]][g % 2]
                prev_ev = self.pt_free[g % 2]
                for j in range(4):
                    hh = g * 4 + j
                    w = mx(([tv_rope, t_qb] + bar if hh == 0 else []) +
                           ([prev_ev] if j == 0 else []))
                    inc = [self.sPE.inc()] if j == 3 else []
                    self.tr(pt[:, j * 128:(j + 1) * 128], self.xnb[:, hh * 128:(hh + 1) * 128],
                            waits=w, incs=inc)
                t_tr = self.sPE.now()
                t_ev = self.sVE.inc()
                dst = self.qcatT.rearrange("p (h t) -> p h t", h=H)[:, g * 4:(g + 1) * 4,
                                                                   tt * 128:(tt + 1) * 128]
                self.vcopy(dst, pt[:, 0:512].rearrange("p (h t) -> p h t", h=4),
                           waits=mx([t_tr] + (bar if (tt == 0 and g == 0) else [])),
                           incs=[t_ev])
                self.pt_free[g % 2] = t_ev
            self.t_xnb_free = self.sPE.now()
        t_qcat = self.sVE.now()
        self.tap("tap_qcatT%d" % ch, self.qcatT, [t_qcat])

        bar2 = self.barrier()
        t_wkvb = self.dma("gpsimd", self.wkvb, d["wkvb_p"], self.dG, waits=bar2)
        t_wkvb_f = None
        for rt in range(RT_KV):
            t_wkvb_f = self.sVE.inc()
            self.vts(self.wkvb[:, rt * 3072:(rt + 1) * 3072],
                     self.wkvb[:, rt * 3072:(rt + 1) * 3072],
                     self.nrmkv[:, rt:rt + 1], None, ALU.mult, ALU.bypass,
                     waits=[t_wkvb, self.t_nkv], incs=[t_wkvb_f])
        t_kpe_ld = self.dma("sync", self.kpeT, self.kpeTs, self.dS,
                            waits=mx(bar2 + [self.t_kpe_spill]))
        self.t_ps67_free = [self.pt2_free[0], self.pt2_free[1]]
        t_regt = mx(bar2 + [self.xst_free[0], self.xst_free[1]])
        t_out_toks = []
        st = {}

        def a_units(h):
            u = [("kn", kc) for kc in range(8)] + [("kpe",)] + \
                [("v", g) for g in range(8)]
            return u

        def emit_A(h, unit):
            p = h % 2
            s = st[h]
            kind = unit[0]
            if kind == "kn":
                kc = unit[1]
                half = s["half"] = 1 - s.get("half", 0)
                pk = self.PS67[0:64, half * 512:(half + 1) * 512]
                for rt in range(RT_KV):
                    w = mx([self.t_cnT_done, t_wkvb_f, self.t_ps67_free[half]]
                           + (bar2 if kc == 0 else []))
                    inc = [self.sPE.inc()] if rt == RT_KV - 1 else []
                    self.mm(pk, self.wkvb[:, rt * 3072 + h * 192:rt * 3072 + h * 192 + 64],
                            self.cnT[:, rt * T + kc * 512:rt * T + (kc + 1) * 512],
                            start=(rt == 0), stop=(rt == RT_KV - 1), waits=w, incs=inc)
                t_pk = self.sPE.now()
                tev = self.sSE.inc()
                self.scopy(self.ksT2[p][0:64, kc * 512:(kc + 1) * 512], pk,
                           waits=mx([t_pk] + t_regt), incs=[tev])
                s["ks_ev"].append(tev)
                self.t_ps67_free[half] = tev
            elif kind == "kpe":
                tev = self.sVE.inc()
                self.vcopy(self.ksT2[p][64:128, :], self.kpeT[0:64, :],
                           waits=mx([t_kpe_ld, s["ks_ev"][-1]] + t_regt), incs=[tev])
                s["kpe"] = tev
            elif kind == "v":
                g = unit[1]
                half = s["half"] = 1 - s.get("half", 0)
                for j in range(4):
                    kt = g * 4 + j
                    pv = self.PS67[:, half * 512 + j * 128:half * 512 + (j + 1) * 128]
                    for rt in range(RT_KV):
                        w = mx([self.t_ps67_free[half]] if (j == 0 and rt == 0) else [])
                        inc = [self.sPE.inc()] if (rt == RT_KV - 1 and j == 3) else []
                        self.mm(pv, self.cnT[:, rt * T + kt * 128:rt * T + (kt + 1) * 128],
                                self.wkvb[:, rt * 3072 + h * 192 + 64:rt * 3072 + (h + 1) * 192],
                                start=(rt == 0), stop=(rt == RT_KV - 1), waits=w, incs=inc)
                t_pv = self.sPE.now()
                tev = self.sSE.inc()
                self.scopy(self.vnat2[p][:, g * 512:(g + 1) * 512],
                           self.PS67[:, half * 512:(half + 1) * 512],
                           waits=mx([t_pv] + t_regt), incs=[tev])
                s["v_ev"].append(tev)
                self.t_ps67_free[half] = tev
        def pT4(qh, kt):
            return self.xn2T[:, qh * 4096 + (kt % 8) * 512:qh * 4096 + (kt % 8 + 1) * 512]

        def emit_C(h, weave):
            p = h % 2
            s = st[h]
            wq = 0
            accb = self.cnb[:, 0:512]
            po = [self.PS45[:, 0:512], self.PS45[:, 512:1024]]
            t_acc0 = []
            for qh in range(2):
                tok = self.sVE.inc()
                self.vmemset(self.accum2[qh], 0.0,
                             waits=mx([self.t_acc_free[qh]] + t_regt), incs=[tok])
                t_acc0.append(tok)
            t_exp, t_acc, t_pvk = {}, {}, {}
            t_accq = [None, None]
            nw = len(weave)
            for kt in range(KT_B):
                kc = kt // 4
                for qh in range(2):
                    ps2 = self.PS[(kt % 2) * 2 + qh]
                    tokS = self.sPE.inc()
                    self.mm(ps2[:, 0:512], self.ksT2[p][:, kt * 128:(kt + 1) * 128],
                            self.qcatT[:, h * CH + qh * 512:h * CH + (qh + 1) * 512],
                            start=True, stop=True,
                            waits=mx(([s["ks_ev"][kc], s["kpe"]] if qh == 0 else []) +
                                     [self.t_sping_free[(kt % 2) * 2 + qh]] +
                                     ([self.t_po_free[qh]] if kt == 0 else [])),
                            incs=[tokS])
                    t_exp[(kt, qh, "S")] = tokS
                if kt >= 2:
                    km = kt - 2
                    for qh in range(2):
                        tokP = self.sPE.inc()
                        self.mm(po[qh][:, 0:512], self.vnat2[p][:, km * 128:(km + 1) * 128],
                                pT4(qh, km), start=(km == 0), stop=False,
                                waits=mx([t_exp[(km, qh)], s["v_ev"][km // 4]] +
                                         ([self.t_po_free[qh]] if km == 0 else [])),
                                incs=[tokP])
                        t_pvk[(km, qh)] = tokP
                for qh in range(2):
                    ps2 = self.PS[(kt % 2) * 2 + qh]
                    tokE = self.sSE.inc()
                    self.act(pT4(qh, kt), ps2[:, 0:512], AF.Exp, scale=SCALE,
                             waits=mx([t_exp[(kt, qh, "S")], t_pvk.get((kt - 8, qh)),
                                       t_acc.get((kt - 8, qh))] +
                                      (t_regt if kt == 0 else [])),
                             incs=[tokE])
                    t_exp[(kt, qh)] = tokE
                    self.t_sping_free[(kt % 2) * 2 + qh] = tokE
                    tokA = self.sVE.inc()
                    self.vtt(self.accum2[qh], self.accum2[qh], pT4(qh, kt), ALU.add,
                             waits=mx([tokE] + ([t_acc0[qh]] if kt == 0 else [])), incs=[tokA])
                    t_acc[(kt, qh)] = tokA
                    t_accq[qh] = tokA
                hi = (nw * (kt + 1)) // KT_B
                while wq < hi:
                    emit_A(h + 1, weave[wq])
                    wq += 1
            t_pv_done = []
            for km in (KT_B - 2, KT_B - 1):
                for qh in range(2):
                    tokP = self.sPE.inc()
                    self.mm(po[qh][:, 0:512], self.vnat2[p][:, km * 128:(km + 1) * 128],
                            pT4(qh, km), start=False, stop=(km == KT_B - 1),
                            waits=mx([t_exp[(km, qh)], s["v_ev"][7]]), incs=[tokP])
                    if km == KT_B - 1:
                        t_pv_done.append(tokP)
            return (h, t_pv_done, t_exp, t_accq)

        def emit_C_denoms(state):
            h, t_pv_done, t_exp, t_accq = state
            p = h % 2
            accb = self.cnb[:, 0:512]
            po = [self.PS45[:, 0:512], self.PS45[:, 512:1024]]
            for qh in range(2):
                t_accb = self.sSE.inc()
                self.scopy(accb, self.accum2[qh],
                           waits=mx([t_accq[qh], self.t_accb_free,
                                     self.t_cnb_free]), incs=[t_accb])
                self.t_acc_free[qh] = t_accb
                pl = self.PS[qh]
                tokL = self.sPE.inc()
                self.mm(pl[0:1, 0:512], self.onescol[:, 0:1], accb,
                        start=True, stop=True,
                        waits=mx([t_accb, self.t_ones, t_exp[(KT_B - 1, qh)],
                                  t_exp[(KT_B - 2, qh)]]),
                        incs=[tokL])
                self.t_accb_free = tokL
                self.t_cnb_free = tokL
                tv_l = self.sVE.inc()
                self.vcopy(self.lrow[0:1, 0:512], pl[0:1, 0:512], waits=[tokL], incs=[tv_l])
                ts_l1 = self.sSE.inc()
                self.act(self.rrow[0:1, 0:512], self.lrow[0:1, 0:512], AF.Ln,
                         waits=[tv_l], incs=[ts_l1])
                ts_l2 = self.sSE.inc()
                self.act(self.rrowb[0:1, 0:512], self.rrow[0:1, 0:512], AF.Exp, scale=-1.0,
                         waits=[ts_l1], incs=[ts_l2])
                tokB = self.sPE.inc()
                self.mm(pl[:, 0:512], self.onesrow[0:1, :], self.rrowb[0:1, 0:512],
                        start=True, stop=True, waits=[ts_l2], incs=[tokB])
                rbc = self.accum2[qh]
                tv_rb = self.sVE.inc()
                self.vcopy(rbc, pl[:, 0:512], waits=[tokB, t_accb], incs=[tv_rb])
                tv_out = self.sVE.inc()
                self.vtt(self.qcatT[:, h * CH + qh * 512:h * CH + (qh + 1) * 512],
                         po[qh][:, 0:512], rbc, ALU.mult,
                         waits=[t_pv_done[qh], tv_rb], incs=[tv_out])
                t_out_toks.append(tv_out)
                self.t_po_free[qh] = tv_out
                self.t_acc_free[qh] = tv_out
                self.t_sping_free[qh] = tv_rb

        st[0] = {"ks_ev": [], "v_ev": []}
        for u in a_units(0):
            emit_A(0, u)
        for h in range(1, H):
            st[h] = {"ks_ev": [], "v_ev": []}
            stc = emit_C(h - 1, a_units(h))
            emit_C_denoms(stc)
        stc = emit_C(H - 1, [])
        emit_C_denoms(stc)
        self.tap("tap_attnT%d" % ch, self.qcatT, [self.sVE.now()])
        return t_out_toks

    def p2b(self, ch, t_out_toks):
        base = ch * CH
        bar = self.barrier()
        self.t_xnb_free2 = [bar[0], bar[0]]
        if t_out_toks:
            self.pt_free = [t_out_toks[-1], t_out_toks[-1]]
        fin4 = self.fin4
        t_slab = {}
        first_dma = [True]

        def issue_wo(tg, fq, g):
            seq = fq * 2 + g
            buf = self.woslab[seq % 2]
            w = mx([self.t_woslab_free[seq % 2]] + (bar if first_dma[0] else []))
            first_dma[0] = False
            t_slab[(tg, fq, g)] = self.dma("gpsimd", buf[:, :],
                                           self.dram["wo_p"][fq, g, :, :], self.dG,
                                           waits=w)

        def wo_group(tg, jobs):
            issue_wo(tg, 0, 0)
            issue_wo(tg, 0, 1)
            evs = [list(self.t_psbank)]
            for fq in range(4):
                bank_tok = [None] * 4
                for g in range(2):
                    seq = fq * 2 + g
                    buf = self.woslab[seq % 2]
                    for tt4 in range(4):
                        tt = tg * 4 + tt4
                        for j in range(8):
                            hv = g * 8 + j
                            w = mx(([t_slab[(tg, fq, g)]] if (tt4 == 0 and j == 0) else []) +
                                   ([evs[fq][tt4]] if (g == 0 and j == 0) else []) +
                                   (mx(t_out_toks) + bar
                                    if (tg == 0 and fq == 0 and g == 0 and tt4 == 0
                                        and j == 0) else []))
                            inc = ([self.sPE.inc()]
                                   if (j == 7 and (g == 1 or tt4 == 3)) else [])
                            self.mm(self.PS[tt4][:, 0:512],
                                    self.qcatT[:, hv * CH + tt * 128:hv * CH + (tt + 1) * 128],
                                    buf[:, j * 512:(j + 1) * 512],
                                    start=(g == 0 and j == 0), stop=(g == 1 and j == 7),
                                    waits=w, incs=inc)
                            if g == 1 and j == 7:
                                bank_tok[tt4] = self.sPE.now()
                    self.t_woslab_free[seq % 2] = self.sPE.now()
                    if seq + 2 < 8:
                        issue_wo(tg, fq + 1, g)
                ev_fq = []
                for tt4 in range(4):
                    tok = self.sSE.inc()
                    fguard = self.t_finbuf4[tt4] if fq == 0 else None
                    fguard = list(fguard) if isinstance(fguard, tuple) else mx([fguard])
                    self.act(fin4[tt4][:, fq * 512:(fq + 1) * 512], self.PS[tt4][:, 0:512],
                             AF.Copy,
                             waits=mx([bank_tok[tt4]] + fguard +
                                      (bar if (tg == 0 and fq == 0) else [])),
                             incs=[tok])
                    ev_fq.append(tok)
                evs.append(ev_fq)
                if jobs:
                    jobs.pop(0)()
            self.t_psbank = evs[-1]
            return evs[-1]

        def ln_tile_a(tt, t_fin_ev):
            par = tt % 2
            fb = fin4[tt % 4]
            scr2 = [self.qf32, self.x1st][par]
            g2 = [self.t_qf32_free, self.t_x1st_free][par]
            rstd, bias, ts, tv = self.ln_stats([(fb, D)], D, EPS_LN, 32 * par,
                                               mx([t_fin_ev, g2]), scratch=scr2)
            t_ln = self.sSE.inc()
            self.act(scr2[:, :], fb, AF.Identity, bias=bias, scale=rstd,
                     waits=mx([ts, tv]), incs=[t_ln])
            self.stats_guard[32 * par] = t_ln
            row0 = base + tt * 128
            t_x = self.dma("sync", self.xst[par][:, :],
                           self.dram["xmy"][row0:row0 + 128, :], self.dS,
                           waits=mx([self.xst_free[par]] + self.t_regt_heads))
            tv_x1 = self.sVE.inc()
            self.vtt(fb, scr2[:, :], self.xst[par][:, :], ALU.add,
                     waits=mx([t_ln, t_x]), incs=[tv_x1])
            self.xst_free[par] = tv_x1
            if par == 0:
                self.t_qf32_free = tv_x1
            else:
                self.t_x1st_free = tv_x1
            t_x1w = self.dma("sync", self.x1s[row0:row0 + 128, :], fb, self.dO,
                             waits=[tv_x1])
            self.t_x1w_last = t_x1w
            rstd2, bias2, ts2, tv2 = self.ln_stats([(fb, D)], D, EPS_LN, 16 + 32 * par,
                                                   [tv_x1], scratch=scr2)
            xnb2 = [self.xnb, self.xnT][par]
            t_xn2 = self.sSE.inc()
            self.act(xnb2[:, :], fb, AF.Identity, bias=bias2, scale=rstd2,
                     waits=mx([ts2, tv2, self.t_xnb_free2[par]]), incs=[t_xn2])
            self.stats_guard[16 + 32 * par] = t_xn2
            self.t_finbuf4[tt % 4] = (t_xn2, t_x1w)
            return t_xn2

        def ln_tile_b(tt, t_xn2):
            par = tt % 2
            xnb2 = [self.xnb, self.xnT][par]
            for g in range(4):
                pt = [self.PS45b[:, 0:1024], self.PS45b[:, 1024:2048]][g % 2]
                prev_ev = self.pt_free[g % 2]
                for j in range(4):
                    dt = g * 4 + j
                    w = mx(([t_xn2, self.t_xn2T_free] if dt == 0 else []) +
                           ([prev_ev] if j == 0 else []))
                    inc = [self.sPE.inc()] if j == 3 else []
                    self.tr(pt[:, j * 128:(j + 1) * 128], xnb2[:, dt * 128:(dt + 1) * 128],
                            waits=w, incs=inc)
                t_tr = self.sPE.now()
                t_ev = self.sVE.inc()
                dst = self.xn2T[:, :].rearrange("p (d t) -> p d t", d=DT)[:, g * 4:(g + 1) * 4,
                                                                         tt * 128:(tt + 1) * 128]
                self.vcopy(dst, pt[:, 0:512].rearrange("p (d t) -> p d t", d=4),
                           waits=mx([t_tr, self.t_xn2T_free]), incs=[t_ev])
                self.pt_free[g % 2] = t_ev
            self.t_xnb_free2[par] = self.sPE.now()

        evs0 = wo_group(0, [])
        xn2_toks = {}

        def mk_a(tt, fev):
            def f():
                xn2_toks[tt] = ln_tile_a(tt, fev)
            return f

        def mk_b(tt):
            def f():
                ln_tile_b(tt, xn2_toks[tt])
            return f
        mk_a(0, evs0[0])()
        mk_a(1, evs0[1])()
        mk_b(0)()
        mk_a(2, evs0[2])()
        mk_b(1)()
        mk_a(3, evs0[3])()
        jobs = [mk_b(2), mk_b(3)]
        evs1 = wo_group(1, jobs)
        while jobs:
            jobs.pop(0)()
        for tt4 in range(4):
            mk_a(4 + tt4, evs1[tt4])()
            if tt4 >= 1:
                mk_b(4 + tt4 - 1)()
        mk_b(7)()
        t_xn2T = self.sVE.now()
        self.tap("tap_xn2T%d" % ch, self.xn2T[:, :], [t_xn2T])
        return t_xn2T

    def p3(self, ch, t_xn2T):
        base = ch * CH
        NT = CH // 128
        bar = self.barrier()
        pg = self.PS67[:, 0:512]
        st = self.stats
        for tt in range(NT):
            for dt in range(DT):
                w = mx([t_xn2T, self.t_gw] +
                       ([self.t_pg_free, self.t_ps67_free[0], self.t_ps67_free[1]]
                        if dt == 0 else []))
                inc = [self.sPE.inc()] if dt == DT - 1 else []
                self.mm(pg[:, 0:8], self.xn2T[:, dt * CH + tt * 128:dt * CH + (tt + 1) * 128],
                        self.gw[:, dt * E:(dt + 1) * E], start=(dt == 0), stop=(dt == DT - 1),
                        waits=w, incs=inc)
            t_g = self.sPE.now()
            tv1 = self.sVE.inc()
            self.vred(st[:, 64:65], pg[:, 0:8], ALU.max,
                      waits=mx([t_g, self.stats_guard.get(64)]), incs=[tv1])
            tv1b = self.sVE.inc()
            self.vts(st[:, 65:66], st[:, 64:65], -1.0, None, ALU.mult, ALU.bypass,
                     waits=[tv1], incs=[tv1b])
            ts_e = self.sSE.inc()
            self.act(self.e8[:, 0:8], pg[:, 0:8], AF.Exp, bias=st[:, 65:66],
                     waits=mx([tv1b, self.t_e8_free]), incs=[ts_e])
            self.t_pg_free = ts_e
            tv1c = self.sVE.inc()
            self.vred(st[:, 66:67], self.e8[:, 0:8], ALU.add, waits=[ts_e], incs=[tv1c])
            tv2 = self.sVE.inc()
            self.vrecip(st[:, 67:68], st[:, 66:67], waits=[tv1c], incs=[tv2])
            tv3 = self.sVE.inc()
            self.vts(self.combtmp[:, 0:8], self.e8[:, 0:8], st[:, 67:68], None, ALU.mult,
                     ALU.bypass, waits=[tv2], incs=[tv3])
            self.t_e8_free = tv3
            self.stats_guard[64] = tv3
            tv4 = self.sVE.inc()
            self.vred(st[:, 68:69], self.combtmp[:, 0:8], ALU.max, waits=[tv3], incs=[tv4])
            tv5 = self.sVE.inc()
            self.vts(self.tmp8[:, 0:8], self.combtmp[:, 0:8], st[:, 68:69], -1e30,
                     ALU.is_ge, ALU.mult, waits=mx([tv4]), incs=[tv5])
            tv6 = self.sVE.inc()
            self.vtt(self.tmp8[:, 0:8], self.combtmp[:, 0:8], self.tmp8[:, 0:8], ALU.add,
                     waits=[tv5], incs=[tv6])
            tv7 = self.sVE.inc()
            self.vred(st[:, 69:70], self.tmp8[:, 0:8], ALU.max, waits=[tv6], incs=[tv7])
            tv8 = self.sVE.inc()
            self.vts(self.tmp8[:, 0:8], self.combtmp[:, 0:8], st[:, 69:70], None,
                     ALU.is_ge, ALU.bypass, waits=[tv7], incs=[tv8])
            tv9 = self.sVE.inc()
            self.vtt(self.comb[:, tt * E:(tt + 1) * E], self.combtmp[:, 0:8],
                     self.tmp8[:, 0:8], ALU.mult, waits=[tv8], incs=[tv9])
        t_comb = self.sVE.now()
        self.tap("tap_comb%d" % ch, self.comb[:, :], [t_comb])

        t_y0 = self.sVE.inc()
        self.vmemset(self.Y, 0.0, waits=mx([self.t_Y_free, self.t_cnT_spill] + bar),
                     incs=[t_y0])
        t_Y_tile = {}

        specs = [("wgu_p", "wd_p", e, True) for e in range(E)] + \
                [("swgu_p", "swd_p", s, False) for s in range(2)]
        gu_tok, wd_tok = {}, {}

        def issue_gu(si, ht):
            gname, _dn, ei, _r = specs[si]
            buf = self.wgu[ht % 2]
            w = mx([self.t_wgu_free[ht % 2]] + (bar if (si == 0 and ht < 2) else []))
            tg = self.dma("gpsimd", buf.rearrange("p (s f) -> p s f", s=2),
                          self.dram[gname][ei, ht, :, :, :].rearrange("s p f -> p s f"),
                          self.dG, waits=w)
            gu_tok[(si, ht)] = tg

        def issue_wd(si, fc):
            _gn, dname, ei, _r = specs[si]
            buf = self.wds[fc % 2]
            wd_tok[(si, fc)] = self.dma(
                "gpsimd", buf.rearrange("p (h f) -> p h f", h=HT),
                self.dram[dname][ei, fc, :, :].rearrange("(h p) f -> p h f", p=128),
                self.dG, waits=mx([self.t_wds_free[fc % 2]] +
                                  (bar if (si == 0 and fc < 2) else [])))

        issue_gu(0, 0)
        issue_gu(0, 1)
        for si, (gname, dname, ei, routed) in enumerate(specs):
            t_gu = {ht: (gu_tok[(si, ht)], gu_tok[(si, ht)]) for ht in (0, 1)}
            for ht in range(HT):
                bg = self.wgu[ht % 2][:, 0:2048]
                bu = self.wgu[ht % 2][:, 2048:4096]
                pgt = [self.PS[0], self.PS[1]]
                pup = [self.PS[2], self.PS[3]]
                for nh in range(2):
                    for dt in range(DT):
                        w = mx([t_gu[ht][0], t_xn2T] +
                               ([self.t_psg_free, self.t_psu_free]
                                if (dt == 0 and nh == 0) else []))
                        self.mm(pgt[nh][:, 0:512], bg[:, dt * 128:(dt + 1) * 128],
                                self.xn2T[:, dt * CH + nh * 512:dt * CH + (nh + 1) * 512],
                                start=(dt == 0), stop=(dt == DT - 1), waits=w)
                        w2 = mx([t_gu[ht][1]] if (dt == 0 and nh == 0) else [])
                        inc = [self.sPE.inc()] if (dt == DT - 1 and nh == 1) else []
                        self.mm(pup[nh][:, 0:512], bu[:, dt * 128:(dt + 1) * 128],
                                self.xn2T[:, dt * CH + nh * 512:dt * CH + (nh + 1) * 512],
                                start=(dt == 0), stop=(dt == DT - 1), waits=w2, incs=inc)
                t_G = self.sPE.now()
                self.t_wgu_free[ht % 2] = t_G
                if ht + 2 < HT:
                    issue_gu(si, ht + 2)
                    t_gu[ht + 2] = (gu_tok[(si, ht + 2)], gu_tok[(si, ht + 2)])
                elif ht == HT - 1:
                    issue_wd(si, 0)
                    issue_wd(si, 1)
                ts_a = self.sSE.inc()
                self.act(self.sg[:, 0:512], pgt[0][:, 0:512], AF.Silu,
                         waits=mx([t_G, self.t_sg_free] + bar), incs=[ts_a])
                ts_b = self.sSE.inc()
                self.act(self.sg[:, 512:1024], pgt[1][:, 0:512], AF.Silu, incs=[ts_b])
                tva = self.sVE.inc()
                self.vstt(self.Gs[:, ht * CH:ht * CH + 512], pup[0][:, 0:512], 1.0,
                          self.sg[:, 0:512], ALU.mult, ALU.mult, waits=[ts_a], incs=[tva])
                tvb = self.sVE.inc()
                self.vstt(self.Gs[:, ht * CH + 512:(ht + 1) * CH], pup[1][:, 0:512], 1.0,
                          self.sg[:, 512:1024], ALU.mult, ALU.mult, waits=[ts_b], incs=[tvb])
                self.t_psg_free = ts_b
                self.t_psu_free = tvb
                self.t_sg_free = tvb
            t_Gs = self.sVE.now()
            t_wd = {fc: wd_tok[(si, fc)] for fc in (0, 1)}
            for fc in range(4):
                buf = self.wds[fc % 2]
                for tt in range(NT):
                    pdx = [self.PS45[:, 0:512], self.PS45[:, 512:1024]][tt % 2]
                    for ht in range(HT):
                        w = mx([t_wd[fc], t_Gs] + ([self.t_pd_free[tt % 2]] if ht == 0 else []))
                        inc = [self.sPE.inc()] if ht == HT - 1 else []
                        self.mm(pdx[:, 0:512],
                                self.Gs[:, ht * CH + tt * 128:ht * CH + (tt + 1) * 128],
                                buf[:, ht * 512:(ht + 1) * 512],
                                start=(ht == 0), stop=(ht == HT - 1), waits=w, incs=inc)
                    t_d = self.sPE.now()
                    yap = self.Y[:, tt * D + fc * 512:tt * D + (fc + 1) * 512]
                    tv_y = self.sVE.inc()
                    if routed:
                        self.vstt(yap, pdx[:, 0:512], self.comb[:, tt * E + ei:tt * E + ei + 1],
                                  yap, ALU.mult, ALU.add, waits=mx([t_d, t_comb, t_y0]),
                                  incs=[tv_y])
                    else:
                        self.vtt(yap, yap, pdx[:, 0:512], ALU.add, waits=mx([t_d, t_y0]),
                                 incs=[tv_y])
                    self.t_pd_free[tt % 2] = tv_y
                    t_Y_tile[tt] = tv_y
                self.t_wds_free[fc % 2] = self.sPE.now()
                if fc + 2 < 4:
                    issue_wd(si, fc + 2)
                    t_wd[fc + 2] = wd_tok[(si, fc + 2)]
                if fc == 1 and si + 1 < len(specs):
                    issue_gu(si + 1, 0)
                    issue_gu(si + 1, 1)
        t_Y = self.sVE.now()
        self.tap("tap_Y%d" % ch, self.Y, [t_Y])

        xbuf = [self.x1st, self.qf32]
        scpar = [self.qf32, self.x1st]
        t_xb_free = [self.t_x1st_free, self.t_qf32_free]
        for tt in range(NT):
            par = tt % 2
            yap = self.Y[:, tt * D:(tt + 1) * D]
            rstd, bias, ts, tv = self.ln_stats([(yap, D)], D, EPS_LN, 32 * par,
                                               mx([t_Y_tile[tt], t_xb_free[1 - par]]),
                                               scratch=scpar[par])
            t_ln = self.sSE.inc()
            self.act(yap, yap, AF.Identity, bias=bias, scale=rstd,
                     waits=mx([ts, tv]), incs=[t_ln])
            self.stats_guard[32 * par] = t_ln
            t_x1 = self.dma("sync", xbuf[par][:, :],
                            self.x1s[base + tt * 128:base + (tt + 1) * 128, :], self.dS,
                            waits=mx([t_xb_free[par], self.t_x1w_last]))
            tv_o = self.sVE.inc()
            self.vtt(xbuf[par][:, :], xbuf[par][:, :], yap, ALU.add,
                     waits=mx([t_ln, t_x1]), incs=[tv_o])
            t_st = self.dma("sync", self.out[base + tt * 128:base + (tt + 1) * 128, :],
                            xbuf[par][:, :], self.dO, waits=[tv_o])
            t_xb_free[par] = t_st
        self.t_x1st_free = t_xb_free[0]
        self.t_qf32_free = t_xb_free[1]
        self.t_Y_free = self.sVE.now()
        self.t_xn2T_free = self.sPE.now()
        if ch + 1 < NCH:
            self.t_cnT_done = self.dma("sync", self.cnT, self.cnTs, self.dS,
                                       waits=mx([self.t_cnT_spill, self.t_Y_free]))

    def build(self):
        self.stats_guard = {}
        self.xst_free = [None, None]
        self.pt_free = [None, None]
        for name in ("t_pc_free t_c16_free "
                     "t_x1st_free t_pg_free t_Y_free t_psg_free "
                     "t_psu_free t_sg_free t_cnb_free t_xnb_free t_qf32_free "
                     "t_ropetmp_free t_x1w_last "
                     "t_xn2T_free t_e8_free t_cnT_spill t_kpe_spill t_xnT_free").split():
            setattr(self, name, None)
        self.pt2_free = [None, None]
        self.t_cnb_free2 = [None, None]
        self.t_rms_scr = None
        self.t_ps67_free = [None, None]
        self.t_sping_free = [None, None, None, None]
        self.t_po_free = [None, None]
        self.t_acc_free = [None, None]
        self.t_accb_free = None
        self.t_cnb_free = None
        self.t_pq_free = [None, None]
        self.t_woslab_free = [None, None]
        self.t_finbuf4 = [None, None, None, None]
        self.t_psbank = [None, None, None, None]
        self.t_wgu_free = [None, None]
        self.t_wds_free = [None, None]
        self.t_pd_free = [None, None]
        self.t_regt_heads = []

        self.p0()
        self.p1()
        if _PHASES != "P1":
            for ch in range(NCH):
                bar = self.barrier()
                t_out = self.p2(ch, bar)
                if _PHASES == "P2" and ch == 0:
                    break
                self.t_regt_heads = self.barrier()
                t_xn2T = self.p2b(ch, t_out)
                if _PHASES == "P2B" and ch == 0:
                    break
                self.p3(ch, t_xn2T)
        self.P.emit("sync", lambda e: e.nop(),
                    waits=[self.dO.now(), self.dS.now(), self.dG.now()])
        block = self.es.enter_context(self.nc.Block())
        self.P.install(block)
        self.es.close()
        return self.nc


def _prep_weights(inp):
    f = np.ascontiguousarray
    w = {}
    w["wqa_p"] = f(inp["L0_attn_wq_a"].reshape(DT, 128, QR).transpose(1, 0, 2).reshape(128, DT * QR))
    w["wqb_p"] = f(inp["L0_attn_wq_b"].reshape(RT_Q, 128, D).transpose(1, 0, 2).reshape(128, RT_Q * D))
    w["qnorm_p"] = f(inp["L0_attn_q_a_norm_w"].reshape(RT_Q, 128).T)
    w["wkva_p"] = f(inp["L0_attn_wkv_a"].reshape(DT, 128, 576).transpose(1, 0, 2).reshape(128, DT * 576))
    w["wkvb_p"] = f(inp["L0_attn_wkv_b"].reshape(RT_KV, 128, 3072).transpose(1, 0, 2).reshape(128, RT_KV * 3072))
    w["kvnorm_p"] = f(inp["L0_attn_kv_a_norm_w"].reshape(RT_KV, 128).T)
    w["wo_p"] = f(inp["L0_attn_wo"].reshape(2, 8, 128, 4, 512)
                  .transpose(3, 0, 2, 1, 4).reshape(4, 2, 128, 4096))
    w["gw_p"] = f(inp["L0_ffn_gate_w"].reshape(DT, 128, E).transpose(1, 0, 2).reshape(128, DT * E))
    wg = inp["L0_ffn_w_gate"].reshape(E, DT, 128, HT, 128).transpose(0, 3, 2, 1, 4) \
        .reshape(E, HT, 128, DT * 128)
    wu = inp["L0_ffn_w_up"].reshape(E, DT, 128, HT, 128).transpose(0, 3, 2, 1, 4) \
        .reshape(E, HT, 128, DT * 128)
    w["wgu_p"] = f(np.stack([wg, wu], axis=2))
    w["wd_p"] = f(inp["L0_ffn_w_down"].reshape(E, HT, 128, 4, 512).transpose(0, 3, 1, 2, 4)
                  .reshape(E, 4, HT * 128, 512))
    swg = inp["L0_ffn_sw_gate"].reshape(DT, 128, 2, HT, 128) \
        .transpose(2, 3, 1, 0, 4).reshape(2, HT, 128, DT * 128)
    swu = inp["L0_ffn_sw_up"].reshape(DT, 128, 2, HT, 128) \
        .transpose(2, 3, 1, 0, 4).reshape(2, HT, 128, DT * 128)
    w["swgu_p"] = f(np.stack([swg, swu], axis=2))
    swd = inp["L0_ffn_sw_down"].reshape(2, HT, 128, 4, 512)
    w["swd_p"] = f(swd.transpose(0, 3, 1, 2, 4).reshape(2, 4, HT * 128, 512))
    return w


def _rope_tables():
    inv_freq = 1.0 / 10000.0 ** (np.arange(0, ROPE, 2, dtype=np.float32) / ROPE)
    t = np.arange(T, dtype=np.float32)
    freqs = np.outer(t, inv_freq)
    return np.cos(freqs).astype(np.float32), np.sin(freqs).astype(np.float32)


def build_in_maps(inputs):
    x = np.asarray(inputs["L0_attn_x"], dtype=np.float32)
    w = _prep_weights({k: np.asarray(v, dtype=np.float32) for k, v in inputs.items()
                       if k != "L0_attn_x"})
    cos, sin = _rope_tables()
    consts = dict(ident=np.eye(128, dtype=BF),
                  onesrow=np.ones((1, 128), dtype=BF),
                  onesneg=np.full((1, 128), -1.0, dtype=BF),
                  onescol=np.ones((128, 1), dtype=BF),
                  cosb=np.ascontiguousarray(cos.reshape(KT_B, 128, 32).transpose(1, 0, 2)
                                            .reshape(128, KT_B * 32)),
                  sinb=np.ascontiguousarray(sin.reshape(KT_B, 128, 32).transpose(1, 0, 2)
                                            .reshape(128, KT_B * 32)))
    in_maps = []
    for core in range(8):
        b, half = core // 2, core % 2
        my0 = half * MY
        c16 = np.ascontiguousarray(
            np.tile(cos[my0:my0 + MY].reshape(MY // 128, 128, 1, 32), (1, 1, 16, 1))
            .reshape(MY // 128, 128, 512))
        s16 = np.ascontiguousarray(
            np.tile(sin[my0:my0 + MY].reshape(MY // 128, 128, 1, 32), (1, 1, 16, 1))
            .reshape(MY // 128, 128, 512))
        m = dict(w)
        m.update(consts)
        m["xb"] = np.ascontiguousarray(x[b])
        m["xmy"] = np.ascontiguousarray(x[b, my0:my0 + MY])
        m["c16"] = c16
        m["s16"] = s16
        in_maps.append(m)
    return in_maps


def kernel(**inputs):
    in_maps = build_in_maps(inputs)
    bld = Builder(_TAPS)
    nc = bld.build()
    res = run_bass_kernel_spmd(nc, in_maps, core_ids=list(range(8)))
    out = np.empty((B, T, D), dtype=np.float32)
    for core in range(8):
        b, half = core // 2, core % 2
        out[b, half * MY:(half + 1) * MY, :] = res.results[core]["out"]
    kernel._last_results = res
    return out


# version 27
# speedup vs baseline: 1.1114x; 1.1114x over previous
import contextlib
import numpy as np
import ml_dtypes
import concourse.bass as bass
import concourse.mybir as mybir
from concourse.bass_utils import run_bass_kernel_spmd

F32 = mybir.dt.float32
BF16 = mybir.dt.bfloat16
AF = mybir.ActivationFunctionType
ALU = mybir.AluOpType
AX = mybir.AxisListType
BF = ml_dtypes.bfloat16

B, T, D = 4, 4096, 2048
H, DK, NOPE, ROPE, VD = 16, 128, 64, 64, 128
KVR, QR = 512, 768
E, HF, HS = 8, 1408, 2816
MY, CH = 2048, 1024
NCH = MY // CH
SCALE = DK ** -0.5
EPS_RMS = float(np.finfo(np.float32).eps)
EPS_LN = 1e-5
KT_B = T // 128
DT = D // 128
RT_KV = KVR // 128
RT_Q = QR // 128
HT = HF // 128

_TAPS = {}
_PHASES = "ALL"

ENGINES = ("sync", "gpsimd", "scalar", "vector", "tensor")


class Sem:
    __slots__ = ("h", "v")

    def __init__(self, es, nc, name):
        self.h = es.enter_context(nc.semaphore(name))
        self.v = 0

    def inc(self, n=1):
        self.v += n
        return (self.h, self.v, n)

    def now(self):
        return (self.h, self.v, 0)


def mx(toks):
    return [t for t in toks if t is not None]


class Prog:
    def __init__(self):
        self.ops = {e: [] for e in ENGINES}
        self.hwm = {e: {} for e in ENGINES}
        self.n = 0

    def emit(self, eng, fn, waits=(), incs=()):
        real = []
        for tok in waits:
            if tok is None:
                continue
            h, v, _n = tok
            if v <= 0:
                continue
            key = id(h)
            if self.hwm[eng].get(key, -1) < v:
                self.hwm[eng][key] = v
                real.append((h, v))
        self.ops[eng].append((real, fn, [t for t in incs if t is not None]))
        self.n += 1

    def install(self, block):
        for eng in ENGINES:
            ops = self.ops[eng]
            if not ops:
                continue
            deco = getattr(block, eng)

            def make(ops=ops):
                def _(e):
                    for waits, fn, incs in ops:
                        for (h, v) in waits:
                            e.wait_ge(h, v)
                        bi = fn(e)
                        for (h, _v, n) in incs:
                            bi = bi.then_inc(h, n)
                return _
            deco(make())


class Builder:
    def __init__(self, taps):
        self.taps = dict(taps)
        nc = bass.Bass("TRN2", target_bir_lowering=False, debug=False)
        self.nc = nc
        self.es = contextlib.ExitStack()
        self.P = Prog()
        es = self.es
        dram = {}

        def din(name, shape, dt=F32):
            dram[name] = nc.dram_tensor(name, list(shape), dt, kind="ExternalInput").ap()
        din("xb", [T, D])
        din("xmy", [MY, D])
        din("wqa_p", [128, DT * QR])
        din("wqb_p", [128, RT_Q * D])
        din("qnorm_p", [128, RT_Q])
        din("wkva_p", [128, DT * 576])
        din("wkvb_p", [128, RT_KV * 3072])
        din("kvnorm_p", [128, RT_KV])
        din("wo_p", [4, 2, 128, 4096])
        din("gw_p", [128, DT * E])
        din("wgu_p", [E, HT, 2, 128, DT * 128])
        din("wd_p", [E, 4, HT * 128, 512])
        din("swgu_p", [2, HT, 2, 128, DT * 128])
        din("swd_p", [2, 4, HT * 128, 512])
        din("cosb", [128, KT_B * 32])
        din("sinb", [128, KT_B * 32])
        din("c16", [MY // 128, 128, 512])
        din("s16", [MY // 128, 128, 512])
        din("ident", [128, 128], BF16)
        din("onesrow", [1, 128], BF16)
        din("onesneg", [1, 128], BF16)
        din("onescol", [128, 1], BF16)
        self.dram = dram
        self.out = nc.dram_tensor("out", [MY, D], F32, kind="ExternalOutput").ap()
        self.x1s = nc.dram_tensor("x1s", [MY, D], F32, kind="Internal").ap()
        self.cnTs = nc.dram_tensor("cnTs", [128, RT_KV * T], BF16, kind="Internal").ap()
        self.kpeTs = nc.dram_tensor("kpeTs", [64, T], BF16, kind="Internal").ap()
        self.tap_out = {}
        for tname, (shape, dt) in self.taps.items():
            self.tap_out[tname] = nc.dram_tensor(tname, list(shape), dt,
                                                 kind="ExternalOutput").ap()

        def sb(name, shape, dt):
            return es.enter_context(nc.sbuf_tensor(name, list(shape), dt))
        arw = sb("arw", [128, 24576], BF16)
        arbig = sb("arbig", [128, 32768], BF16)
        regt = sb("regt", [128, 6144], F32)
        arwF = arw[:, :].bitcast(F32)
        arbF = arbig[:, :].bitcast(F32)
        regtB = regt[:, :].bitcast(BF16)
        self.wkva = arw[:, 0:DT * 576]
        self._cnb2b_view = arw[:, 17408:18176]
        self.arw_rms_scr = arwF[:, 9088:9600]
        self._xnbr0 = arw[:, 13312:15360]
        self._xnbr1 = arw[:, 15360:17408]
        self.cosb = arwF[:, 4608:4608 + KT_B * 32]
        self.sinb = arwF[:, 4608 + KT_B * 32:4608 + 2 * KT_B * 32]
        self.kpeT = arw[0:64, 20480:24576]
        self.wqa = arw[:, 0:DT * QR]
        self.wqb = arw[:, DT * QR:DT * QR + RT_Q * D]
        self.wkvb = arw[:, 0:RT_KV * 3072]
        self.ksT = arw[:, 12288:16384]
        self.vnat = arw[:, 16384:20480]
        self.woslab = [arw[:, 0:4096], arw[:, 4096:8192]]
        self.fin4 = [arwF[:, 4096 + i * 2048:4096 + (i + 1) * 2048] for i in range(4)]
        self.wgu = [arw[:, 0:4096], arw[:, 4096:8192], arw[:, 19456:23552]]
        self.wds = [arw[:, 8192:8192 + HT * 512], arw[:, 8192 + HT * 512:8192 + 2 * HT * 512]]
        self.qcatT = arbig[:, 0:H * CH]
        self.cnT = arbig[:, 16384:16384 + RT_KV * T]
        self.Y = arbF[:, 0:(CH // 128) * D]
        self.xst = [regt[:, 0:D], regt[:, D:2 * D]]
        self.c16s = regt[:, 4096:4608]
        self.s16s = regt[:, 4608:5120]
        self.ropetmp = regt[:, 5120:5632]
        self.accum2 = [regt[:, 4096:4608], regt[:, 4608:5120]]
        self._pt2_init_done = False
        self.Gs = regtB[:, 0:HT * CH]
        self.identb = sb("identb", [128, 128], BF16)
        self.onesrow = sb("onesrow_s", [1, 128], BF16)
        self.onesneg = sb("onesneg_s", [1, 128], BF16)
        self.onescol = sb("onescol_s", [128, 1], BF16)
        self.gw = sb("gw_s", [128, DT * E], BF16)
        self.nrmq = sb("nrmq_s", [128, RT_Q], F32)
        self.nrmkv = sb("nrmkv_s", [128, RT_KV], F32)
        self.stats = sb("stats", [128, 104], F32)
        self.macc = sb("macc", [128, 16], F32)
        self.maccb = sb("maccb", [128, 16], BF16)
        self.tmp8 = sb("tmp8", [128, 16], F32)
        self.e8 = sb("e8", [128, 8], F32)
        self.comb = sb("comb", [128, (CH // 128) * E], F32)
        self.combtmp = sb("combtmp", [128, 8], F32)
        self.mrow = sb("mrow", [1, 2 * CH], BF16)
        self.lrow = sb("lrow", [1, 512], F32)
        self.rrow = sb("rrow", [1, 512], F32)
        self.rrowb = sb("rrowb", [1, 512], BF16)
        self.sg = sb("sg", [128, CH], BF16)
        regtB2 = regtB
        self.pT2 = [[regtB2[:, 10240:10752], regtB2[:, 10752:11264], self.sg[:, 0:512]],
                    [regtB2[:, 11264:11776], regtB2[:, 11776:12288], self.sg[:, 512:1024]]]
        self.cnb = sb("cnb", [128, 768], BF16)
        scr = sb("scr", [128, 8192], BF16)
        self.xnb = scr[:, 0:2048]
        self.xnT = scr[:, 2048:4096]
        self.qf32 = scr[:, :].bitcast(F32)[:, 2048:4096]
        self.ksT2 = [self.ksT, scr[:, 0:4096]]
        self.vnat2 = [self.vnat, scr[:, 4096:8192]]
        self.x1st = sb("x1st", [128, D], F32)
        self.xn2T = sb("xn2T", [128, DT * CH], BF16)

        def ps(name, shape, dt):
            return es.enter_context(nc.psum_tensor(name, list(shape), dt))
        self.PS = [ps("psb%d" % i, [128, 512], F32) for i in range(4)]
        self.PS45 = ps("ps45", [128, 1024], F32)
        self.PS67 = ps("ps67", [128, 1024], F32)
        self.PS45b = self.PS45[:, :].bitcast(BF16)
        self.PS67b = self.PS67[:, :].bitcast(BF16)

        self.dS = Sem(es, nc, "dS")
        self.dG = Sem(es, nc, "dG")
        self.dO = Sem(es, nc, "dO")
        self.sPE = Sem(es, nc, "sPE")
        self.sVE = Sem(es, nc, "sVE")
        self.sSE = Sem(es, nc, "sSE")

    def mm(self, out, lhsT, rhs, start, stop, waits=(), incs=()):
        self.P.emit("tensor", (lambda e, o=out, l=lhsT, r=rhs, s=start, st=stop:
                               e.matmul(o, l, r, start=s, stop=st)), waits, incs)

    def tr(self, out, in_, waits=(), incs=()):
        self.P.emit("tensor", (lambda e, o=out, i=in_, idn=self.identb[:, :]:
                               e.transpose(o, i, idn)), waits, incs)

    def act(self, out, in_, func, bias=0.0, scale=1.0, accum=None, waits=(), incs=()):
        self.P.emit("scalar", (lambda e, o=out, i=in_, f=func, b=bias, s=scale, a=accum:
                               e.activation(o, i, f, bias=b, scale=s, accum_out=a)), waits, incs)

    def scopy(self, out, in_, waits=(), incs=()):
        self.P.emit("scalar", (lambda e, o=out, i=in_: e.copy(o, i)), waits, incs)

    def vcopy(self, out, in_, waits=(), incs=()):
        self.P.emit("vector", (lambda e, o=out, i=in_: e.tensor_copy(o, i)), waits, incs)

    def vtt(self, out, in0, in1, op, waits=(), incs=()):
        self.P.emit("vector", (lambda e, o=out, a=in0, b=in1, p=op:
                               e.tensor_tensor(o, a, b, p)), waits, incs)

    def vts(self, out, in0, s1, s2, op0, op1, waits=(), incs=()):
        self.P.emit("vector", (lambda e, o=out, a=in0, x=s1, y=s2, p0=op0, p1=op1:
                               e.tensor_scalar(o, a, x, y, p0, p1)), waits, incs)

    def vstt(self, out, in0, scalar, in1, op0, op1, waits=(), incs=()):
        self.P.emit("vector", (lambda e, o=out, a=in0, s=scalar, b=in1, p0=op0, p1=op1:
                               e.scalar_tensor_tensor(o, a, s, b, p0, p1)), waits, incs)

    def vred(self, out, in_, op, waits=(), incs=()):
        self.P.emit("vector", (lambda e, o=out, i=in_, p=op:
                               e.tensor_reduce(o, i, AX.X, p)), waits, incs)

    def vmemset(self, ap, val, waits=(), incs=()):
        self.P.emit("vector", (lambda e, a=ap, v=val: e.memset(a, v)), waits, incs)

    def vrecip(self, out, in_, waits=(), incs=()):
        self.P.emit("vector", (lambda e, o=out, i=in_: e.reciprocal(o, i)), waits, incs)

    def dma(self, queue, out, in_, sem, waits=()):
        tok = sem.inc(16)
        self.P.emit(queue, (lambda e, o=out, i=in_: e.dma_start(out=o, in_=i)), waits, [tok])
        return tok

    def tap(self, name, src_ap, waits):
        if name in self.tap_out:
            self.dma("sync", self.tap_out[name], src_ap, self.dO, waits)

    def barrier(self):
        return [self.sPE.now(), self.sVE.now(), self.sSE.now()]

    def ln_stats(self, srcs, n_feat, eps, base, waits, scratch=None):
        st = self.stats
        g = self.stats_guard.get(base)
        if scratch is None:
            scratch = self.qf32
            gq = self.t_qf32_free
        else:
            gq = None
        rstd = st[:, base:base + 1]
        bias = st[:, base + 1:base + 2]
        s_sum = st[:, base + 2:base + 3]
        s_sq = st[:, base + 3:base + 4]
        s_var = st[:, base + 4:base + 5]
        s_m = st[:, base + 5:base + 6]
        s_msq = st[:, base + 6:base + 7]
        vparts, sparts = [], []
        off = 0
        for j, (ap, w) in enumerate(srcs):
            pcol = st[:, base + 8 + j:base + 9 + j]
            tv = self.sVE.inc()
            self.vred(pcol, ap, ALU.add, waits=mx(list(waits) + [g]), incs=[tv])
            vparts.append((pcol, tv))
            psq = st[:, base + 12 + j:base + 13 + j]
            ts = self.sSE.inc()
            self.act(scratch[:, off:off + w], ap, AF.Square,
                     waits=mx(list(waits) + [g, gq]), incs=[ts])
            tvq = self.sVE.inc()
            self.vred(psq, scratch[:, off:off + w], ALU.add, waits=[ts], incs=[tvq])
            off += w
            sparts.append((psq, tvq))
        if scratch is self.qf32:
            self.t_qf32_free = sparts[-1][1]
        c1 = self.sVE.inc()
        if len(srcs) == 1:
            self.vcopy(s_sum, vparts[0][0], waits=[vparts[0][1]], incs=[c1])
        else:
            self.vtt(s_sum, vparts[0][0], vparts[1][0], ALU.add,
                     waits=[t for _, t in vparts], incs=([c1] if len(srcs) == 2 else []))
            for i, (pcol, _t) in enumerate(vparts[2:]):
                last = (i == len(vparts) - 3)
                self.vtt(s_sum, s_sum, pcol, ALU.add, waits=[self.sVE.now()],
                         incs=([c1] if last else []))
        c2 = self.sVE.inc()
        if len(srcs) == 1:
            self.vcopy(s_sq, sparts[0][0], waits=[sparts[0][1]], incs=[c2])
        else:
            self.vtt(s_sq, sparts[0][0], sparts[1][0], ALU.add,
                     waits=[t for _, t in sparts], incs=([c2] if len(srcs) == 2 else []))
            for i, (psq, _t) in enumerate(sparts[2:]):
                last = (i == len(sparts) - 3)
                self.vtt(s_sq, s_sq, psq, ALU.add, waits=[self.sVE.now()],
                         incs=([c2] if last else []))
        m1 = self.sVE.inc()
        self.vts(s_m, s_sum, 1.0 / n_feat, None, ALU.mult, ALU.bypass, waits=[c1], incs=[m1])
        v1 = self.sVE.inc()
        self.vts(s_var, s_sq, 1.0 / n_feat, float(eps), ALU.mult, ALU.add, waits=[c2],
                 incs=[v1])
        mm2 = self.sVE.inc()
        self.vtt(s_msq, s_m, s_m, ALU.mult, waits=[m1], incs=[mm2])
        v2 = self.sVE.inc()
        self.vtt(s_var, s_var, s_msq, ALU.subtract, waits=[v1, mm2], incs=[v2])
        ts1 = self.sSE.inc()
        self.act(st[:, base + 7:base + 8], s_var, AF.Ln, waits=[v2], incs=[ts1])
        ts2 = self.sSE.inc()
        self.act(rstd, st[:, base + 7:base + 8], AF.Exp, scale=-0.5, waits=[ts1], incs=[ts2])
        tv3 = self.sVE.inc()
        self.vstt(bias, s_m, -1.0, rstd, ALU.mult, ALU.mult, waits=[ts2, m1], incs=[tv3])
        return rstd, bias, ts2, tv3

    def rms_rstd(self, sq_parts, n_feat, eps, base):
        st = self.stats
        g = self.stats_guard.get(base)
        rstd = st[:, base:base + 1]
        s_sq = st[:, base + 1:base + 2]
        if len(sq_parts) == 1:
            tv = self.sVE.inc()
            self.vts(s_sq, sq_parts[0][0], 1.0 / n_feat, float(eps), ALU.mult, ALU.add,
                     waits=mx([sq_parts[0][1], g]), incs=[tv])
        else:
            c1 = self.sVE.inc()
            self.vtt(s_sq, sq_parts[0][0], sq_parts[1][0], ALU.add,
                     waits=mx([t for _, t in sq_parts] + [g]), incs=[c1])
            tv = self.sVE.inc()
            self.vts(s_sq, s_sq, 1.0 / n_feat, float(eps), ALU.mult, ALU.add, waits=[c1],
                     incs=[tv])
        ts1 = self.sSE.inc()
        self.act(st[:, base + 2:base + 3], s_sq, AF.Ln, waits=[tv], incs=[ts1])
        ts2 = self.sSE.inc()
        self.act(rstd, st[:, base + 2:base + 3], AF.Exp, scale=-0.5, waits=[ts1], incs=[ts2])
        return rstd, ts2

    def xn_tile(self, src_dram, row0, buf_i, base):
        base = base + 32 * buf_i
        xst = self.xst[buf_i]
        t_x = self.dma("sync", xst[:, :], src_dram[row0:row0 + 128, :], self.dS,
                       waits=mx([self.xst_free[buf_i]]))
        rstd, bias, ts, tv = self.ln_stats([(xst[:, :], D)], D, EPS_LN, base, [t_x])
        t_xn = self.sSE.inc()
        self.act(self.xnb[:, :], xst[:, :], AF.Identity, bias=bias, scale=rstd,
                 waits=mx([ts, tv, self.t_xnb_free]), incs=[t_xn])
        self.stats_guard[base] = t_xn
        self.xst_free[buf_i] = t_xn
        ev_toks = []
        for g in range(4):
            pt = [self.PS45b[:, 0:1024], self.PS45b[:, 1024:2048]][g % 2]
            prev_ev = self.pt_free[g % 2]
            for j in range(4):
                dt = g * 4 + j
                w = mx(([t_xn, self.t_ident] if dt == 0 else []) + ([prev_ev] if j == 0 else []))
                inc = [self.sPE.inc()] if j == 3 else []
                self.tr(pt[:, j * 128:(j + 1) * 128], self.xnb[:, dt * 128:(dt + 1) * 128],
                        waits=w, incs=inc)
            t_tr = self.sPE.now()
            t_ev = self.sVE.inc()
            self.vcopy(self.xnT[:, g * 512:(g + 1) * 512], pt[:, 0:512],
                       waits=mx([t_tr, self.t_xnT_free]), incs=[t_ev])
            self.pt_free[g % 2] = t_ev
            ev_toks.append(t_ev)
        self.t_xnb_free = self.sPE.now()
        return ev_toks, t_xn

    def p0(self):
        d = self.dram
        self.t_ident = self.dma("sync", self.identb[:, :], d["ident"], self.dS)
        self.dma("sync", self.onesrow[:, :], d["onesrow"], self.dS)
        self.dma("sync", self.onesneg[:, :], d["onesneg"], self.dS)
        self.t_ones = self.dma("sync", self.onescol[:, :], d["onescol"], self.dS)
        self.dma("sync", self.cosb, d["cosb"], self.dS)
        self.t_rope = self.dma("sync", self.sinb, d["sinb"], self.dS)
        self.t_wkva = self.dma("gpsimd", self.wkva, d["wkva_p"], self.dG)
        self.t_gw = self.dma("gpsimd", self.gw[:, :], d["gw_p"], self.dG)
        self.t_nq = self.dma("sync", self.nrmq[:, :], d["qnorm_p"], self.dS)
        self.t_nkv = self.dma("sync", self.nrmkv[:, :], d["kvnorm_p"], self.dS)

    def ln_stage_a(self, srcs, base, waits, scratch):
        st = self.stats
        g = self.stats_guard.get(base)
        vparts, sparts = [], []
        off = 0
        for j, (ap, w) in enumerate(srcs):
            pcol = st[:, base + 8 + j:base + 9 + j]
            tv = self.sVE.inc()
            self.vred(pcol, ap, ALU.add, waits=mx(list(waits) + [g]), incs=[tv])
            vparts.append((pcol, tv))
            psq = st[:, base + 12 + j:base + 13 + j]
            ts = self.sSE.inc()
            self.act(scratch[:, off:off + w], ap, AF.Square,
                     waits=mx(list(waits) + [g]), incs=[ts])
            tvq = self.sVE.inc()
            self.vred(psq, scratch[:, off:off + w], ALU.add, waits=[ts], incs=[tvq])
            off += w
            sparts.append((psq, tvq))
        return vparts, sparts

    def ln_stage_b(self, vparts, sparts, n_feat, eps, base):
        st = self.stats
        rstd = st[:, base:base + 1]
        bias = st[:, base + 1:base + 2]
        s_sum = st[:, base + 2:base + 3]
        s_sq = st[:, base + 3:base + 4]
        s_var = st[:, base + 4:base + 5]
        s_m = st[:, base + 5:base + 6]
        s_msq = st[:, base + 6:base + 7]
        c1 = self.sVE.inc()
        if len(vparts) == 1:
            self.vcopy(s_sum, vparts[0][0], waits=[vparts[0][1]], incs=[c1])
        else:
            self.vtt(s_sum, vparts[0][0], vparts[1][0], ALU.add,
                     waits=[t for _, t in vparts], incs=[c1])
        c2 = self.sVE.inc()
        if len(sparts) == 1:
            self.vcopy(s_sq, sparts[0][0], waits=[sparts[0][1]], incs=[c2])
        else:
            self.vtt(s_sq, sparts[0][0], sparts[1][0], ALU.add,
                     waits=[t for _, t in sparts], incs=[c2])
        m1 = self.sVE.inc()
        self.vts(s_m, s_sum, 1.0 / n_feat, None, ALU.mult, ALU.bypass, waits=[c1], incs=[m1])
        v1 = self.sVE.inc()
        self.vts(s_var, s_sq, 1.0 / n_feat, float(eps), ALU.mult, ALU.add, waits=[c2],
                 incs=[v1])
        mm2 = self.sVE.inc()
        self.vtt(s_msq, s_m, s_m, ALU.mult, waits=[m1], incs=[mm2])
        v2 = self.sVE.inc()
        self.vtt(s_var, s_var, s_msq, ALU.subtract, waits=[v1, mm2], incs=[v2])
        ts1 = self.sSE.inc()
        self.act(st[:, base + 7:base + 8], s_var, AF.Ln, waits=[v2], incs=[ts1])
        ts2 = self.sSE.inc()
        self.act(rstd, st[:, base + 7:base + 8], AF.Exp, scale=-0.5, waits=[ts1], incs=[ts2])
        tv3 = self.sVE.inc()
        self.vstt(bias, s_m, -1.0, rstd, ALU.mult, ALU.mult, waits=[ts2, m1], incs=[tv3])
        return rstd, bias, ts2, tv3

    def p1(self):
        arb = self.qcatT
        arbF = arb.bitcast(F32)
        ring = [arbF[:, i * 2048:(i + 1) * 2048] for i in range(4)]
        xnbr = [self._xnbr0, self._xnbr1]
        xnTr = [self.xnb, self.xnT]
        ring_free = [None, None, None, None]
        xnb_free = [None, None]
        xnT_free = [None, None]
        t_pc_free2 = [None, None]
        pc_banks = [(self.PS[0], self.PS[1]), (self.PS[2], self.PS[3])]
        stats_base3 = [0, 32, 16, 48]
        rms_base2 = [(88, 90), (92, 94)]
        stA = {}
        Q = {"b": [], "tr": [], "rms": [], "cn": [], "ctr": []}

        def do_dma(kt):
            i3 = kt % 4
            t_x = self.dma("sync", ring[i3], self.dram["xb"][kt * 128:(kt + 1) * 128, :],
                           self.dS, waits=mx([ring_free[i3]]))
            stA[kt] = t_x

        def do_stageA(kt):
            i3 = kt % 4
            scr = [self.qf32, self.x1st][kt % 2]
            vp, sp = self.ln_stage_a([(ring[i3], D)], stats_base3[i3], [stA[kt]],
                                     scratch=scr)
            stA[kt] = (vp, sp)

        def do_stageB_affine(kt):
            i3, i2 = kt % 4, kt % 2
            vp, sp = stA.pop(kt)
            rstd, bias, ts, tv = self.ln_stage_b(vp, sp, D, EPS_LN, stats_base3[i3])
            t_xn = self.sSE.inc()
            self.act(xnbr[i2][:, :], ring[i3], AF.Identity, bias=bias, scale=rstd,
                     waits=mx([ts, tv, xnb_free[i2]]), incs=[t_xn])
            self.stats_guard[stats_base3[i3]] = t_xn
            ring_free[i3] = t_xn
            Q["tr"].append((kt, t_xn))

        def do_tr_c(kt, t_xn):
            i2 = kt % 2
            ev_toks = []
            for g in range(4):
                pt = [self.PS45b[:, 0:1024], self.PS45b[:, 1024:2048]][g % 2]
                prev_ev = self.pt_free[g % 2]
                for j in range(4):
                    dt = g * 4 + j
                    w = mx(([t_xn, self.t_ident] if dt == 0 else []) +
                           ([prev_ev] if j == 0 else []))
                    inc = [self.sPE.inc()] if j == 3 else []
                    self.tr(pt[:, j * 128:(j + 1) * 128], xnbr[i2][:, dt * 128:(dt + 1) * 128],
                            waits=w, incs=inc)
                t_tr = self.sPE.now()
                t_ev = self.sVE.inc()
                self.vcopy(xnTr[i2][:, g * 512:(g + 1) * 512], pt[:, 0:512],
                           waits=mx([t_tr, xnT_free[i2]]), incs=[t_ev])
                self.pt_free[g % 2] = t_ev
                ev_toks.append(t_ev)
            xnb_free[i2] = self.sPE.now()
            pc0, pc1 = pc_banks[i2]
            for dt in range(DT):
                w = mx([ev_toks[dt // 4], self.t_wkva] +
                       ((t_pc_free2[i2] or []) if dt == 0 else []))
                self.mm(pc0[:, 0:512], xnTr[i2][:, dt * 128:(dt + 1) * 128],
                        self.wkva[:, dt * 576:dt * 576 + 512],
                        start=(dt == 0), stop=(dt == DT - 1), waits=w)
                inc = [self.sPE.inc()] if dt == DT - 1 else []
                self.mm(pc1[:, 0:64], xnTr[i2][:, dt * 128:(dt + 1) * 128],
                        self.wkva[:, dt * 576 + 512:(dt + 1) * 576],
                        start=(dt == 0), stop=(dt == DT - 1), incs=inc)
            t_c = self.sPE.now()
            xnT_free[i2] = t_c
            Q["rms"].append((kt, t_c))

        rms_scr = self.arw_rms_scr

        def do_rms_a(kt, t_c):
            i2 = kt % 2
            rb0, _rb1 = rms_base2[i2]
            pc0 = pc_banks[i2][0]
            ts_sq = self.sSE.inc()
            self.act(rms_scr, pc0[:, 0:512], AF.Square,
                     waits=mx([t_c, self.t_rms_scr]), incs=[ts_sq])
            tv_sq = self.sVE.inc()
            self.vred(self.stats[:, rb0:rb0 + 1], rms_scr, ALU.add,
                      waits=mx([ts_sq, self.stats_guard.get(rb0)]), incs=[tv_sq])
            self.stats_guard[rb0] = tv_sq
            self.t_rms_scr = tv_sq
            Q["cn"].append((kt, t_c, tv_sq))

        def do_cn(kt, t_c, tv_sq):
            i2 = kt % 2
            rb0, rb1 = rms_base2[i2]
            pc0, pc1 = pc_banks[i2]
            rstd, t_rstd = self.rms_rstd([(self.stats[:, rb0:rb0 + 1], tv_sq)], KVR,
                                         EPS_RMS, rb1)
            cnb = self.cnb[:, 0:768] if i2 == 0 else self._cnb2b_view
            t_cn = self.sSE.inc()
            self.act(cnb[:, 0:512], pc0[:, 0:512], AF.Copy, scale=rstd,
                     waits=mx([t_rstd, self.t_cnb_free2[i2]]), incs=[t_cn])
            self.stats_guard[rb1] = t_cn
            cos = self.cosb[:, kt * 32:(kt + 1) * 32]
            sin = self.sinb[:, kt * 32:(kt + 1) * 32]
            x1, x2 = pc1[:, 0:32], pc1[:, 32:64]
            rtp = self.ropetmp[:, 0:64] if i2 == 0 else self.ropetmp[:, 64:128]
            r0, r1 = rtp[:, 0:32], rtp[:, 32:64]
            e1 = self.sVE.inc()
            self.vtt(r0, x1, cos, ALU.mult, waits=mx([self.t_rope, self.t_cnb_free2[i2]]))
            self.vtt(r1, x2, sin, ALU.mult, incs=[e1])
            e2 = self.sVE.inc()
            self.vtt(cnb[:, 512:544], r0, r1, ALU.subtract, waits=[e1], incs=[e2])
            self.vtt(r0, x1, sin, ALU.mult, waits=[e2])
            t_kpe_r = self.sVE.inc()
            self.vtt(r1, x2, cos, ALU.mult, incs=[t_kpe_r])
            t_kpe = self.sVE.inc()
            self.vtt(cnb[:, 544:576], r0, r1, ALU.add, waits=[t_kpe_r], incs=[t_kpe])
            t_pc_free2[i2] = [t_kpe_r, t_cn]
            Q["ctr"].append((kt, t_cn, t_kpe, cnb))

        def do_cn_tr(kt, t_cn, t_kpe, cnb):
            i2 = kt % 2
            ptc = [self.PS67b[:, 0:1024], self.PS67b[:, 1024:2048]][i2]
            prev = self.pt2_free[i2]
            for rt in range(4):
                w = mx([t_cn] + ([prev] if rt == 0 else []))
                self.tr(ptc[:, rt * 128:(rt + 1) * 128],
                        cnb[:, rt * 128:(rt + 1) * 128], waits=w)
            t_trk = self.sPE.inc()
            self.tr(ptc[:, 512:640], cnb[:, 512:640], waits=mx([t_kpe]), incs=[t_trk])
            t_evc = self.sVE.inc()
            dst = self.cnT.rearrange("p (r t) -> p r t",
                                     r=RT_KV)[:, :, kt * 128:(kt + 1) * 128]
            srcv = ptc[:, 0:512].rearrange("p (r t) -> p r t", r=4)
            self.vcopy(dst, srcv, waits=[t_trk], incs=[t_evc])
            t_evk = self.sVE.inc()
            self.vcopy(self.kpeT[0:64, kt * 128:(kt + 1) * 128], ptc[0:64, 512:640],
                       waits=[t_trk], incs=[t_evk])
            self.pt2_free[i2] = t_evk
            self.t_cnb_free2[i2] = self.sPE.now()

        do_dma(0)
        do_dma(1)
        do_dma(2)
        for kt in range(KT_B):
            if kt >= 1:
                do_stageB_affine(kt - 1)
            do_stageA(kt)
            if kt + 3 < KT_B:
                do_dma(kt + 3)
            if len(Q["tr"]) >= 2:
                do_tr_c(*Q["tr"].pop(0))
            if Q["rms"]:
                do_rms_a(*Q["rms"].pop(0))
            if Q["cn"]:
                do_cn(*Q["cn"].pop(0))
            if len(Q["ctr"]) >= 2:
                do_cn_tr(*Q["ctr"].pop(0))
        do_stageB_affine(KT_B - 1)
        for nm in ("tr", "rms", "cn"):
            while Q[nm]:
                if nm == "tr":
                    do_tr_c(*Q["tr"].pop(0))
                    if Q["rms"]:
                        do_rms_a(*Q["rms"].pop(0))
                    if Q["cn"]:
                        do_cn(*Q["cn"].pop(0))
                elif nm == "rms":
                    do_rms_a(*Q["rms"].pop(0))
                    if Q["cn"]:
                        do_cn(*Q["cn"].pop(0))
                else:
                    do_cn(*Q["cn"].pop(0))
        while Q["ctr"]:
            do_cn_tr(*Q["ctr"].pop(0))
        self.t_cnT_done = self.sVE.now()
        self.t_cnT_spill = self.dma("sync", self.cnTs, self.cnT, self.dO,
                                    waits=[self.t_cnT_done])
        self.t_kpe_spill = self.dma("sync", self.kpeTs, self.kpeT, self.dO,
                                    waits=[self.t_cnT_done])
        self.tap("tap_cnT", self.cnT, [self.t_cnT_done])
        self.tap("tap_kpeT", self.kpeT, [self.t_cnT_done])

    def p2(self, ch, bar):
        base = ch * CH
        NT = CH // 128
        d = self.dram
        self.pt_free = [bar[1], bar[1]]
        self.pt2_free = [bar[1], bar[1]]
        self.t_xnb_free = bar[0]
        self.t_xnT_free = bar[0]
        self.t_qf32_free = bar[0]
        if ch > 0:
            self.t_cnT_done = self.dma("sync", self.cnT, self.cnTs, self.dS,
                                       waits=mx(bar + [self.t_cnT_spill]))
        wq_w = (mx([self.t_kpe_spill, self.t_wgu_free[0], self.t_wgu_free[1],
                    self.t_wgu_free[2],
                    self.t_wds_free[0], self.t_wds_free[1], bar[1], bar[2]])
                if ch > 0 else mx(bar + [self.t_kpe_spill]))
        t_wqa = self.dma("gpsimd", self.wqa, d["wqa_p"], self.dG, waits=wq_w)
        t_wqb = self.dma("gpsimd", self.wqb, d["wqb_p"], self.dG)
        t_wqb_f = None
        for rt in range(RT_Q):
            t_wqb_f = self.sVE.inc()
            self.vts(self.wqb[:, rt * D:(rt + 1) * D], self.wqb[:, rt * D:(rt + 1) * D],
                     self.nrmq[:, rt:rt + 1], None, ALU.mult, ALU.bypass,
                     waits=[t_wqb, self.t_nq], incs=[t_wqb_f])
        for tt in range(NT):
            ev_toks, _ = self.xn_tile(self.dram["xmy"], base + tt * 128, tt % 2, 0)
            pqa0, pqa1 = self.PS[0], self.PS[1]
            for dt in range(DT):
                w = mx([ev_toks[dt // 4], t_wqa] + ([self.t_pc_free] if dt == 0 else []))
                self.mm(pqa0[:, 0:512], self.xnT[:, dt * 128:(dt + 1) * 128],
                        self.wqa[:, dt * QR:dt * QR + 512],
                        start=(dt == 0), stop=(dt == DT - 1), waits=w)
                inc = [self.sPE.inc()] if dt == DT - 1 else []
                self.mm(pqa1[:, 0:256], self.xnT[:, dt * 128:(dt + 1) * 128],
                        self.wqa[:, dt * QR + 512:(dt + 1) * QR],
                        start=(dt == 0), stop=(dt == DT - 1), incs=inc)
            t_qa = self.sPE.now()
            ts0 = self.sSE.inc()
            self.act(self.qf32[:, 0:512], pqa0[:, 0:512], AF.Square,
                     waits=mx([t_qa, self.stats_guard.get(80), self.t_qf32_free]), incs=[ts0])
            self.t_qf32_free = None
            ts1 = self.sSE.inc()
            self.act(self.qf32[:, 512:768], pqa1[:, 0:256], AF.Square, incs=[ts1])
            tvs0 = self.sVE.inc()
            self.vred(self.stats[:, 80:81], self.qf32[:, 0:512], ALU.add,
                      waits=mx([ts0, self.stats_guard.get(80)]), incs=[tvs0])
            tvs1 = self.sVE.inc()
            self.vred(self.stats[:, 81:82], self.qf32[:, 512:768], ALU.add,
                      waits=[ts1], incs=[tvs1])
            self.stats_guard[80] = tvs1
            rstd, t_rstd = self.rms_rstd([(self.stats[:, 80:81], tvs0),
                                          (self.stats[:, 81:82], tvs1)], QR, EPS_RMS, 84)
            t_qan0 = self.sSE.inc()
            self.act(self.cnb[:, 0:512], pqa0[:, 0:512], AF.Copy, scale=rstd,
                     waits=mx([t_rstd, self.t_cnb_free]), incs=[t_qan0])
            t_qan = self.sSE.inc()
            self.act(self.cnb[:, 512:768], pqa1[:, 0:256], AF.Copy, scale=rstd, incs=[t_qan])
            self.stats_guard[84] = t_qan
            self.t_pc_free = t_qan
            prev = self.pt2_free[0]
            ptq = self.PS67b[:, 0:2048]
            for j in range(RT_Q):
                w = mx([t_qan0 if j < 4 else t_qan] + ([prev, self.pt2_free[1], t_qa]
                                                      if j == 0 else []))
                inc = [self.sPE.inc()] if j == RT_Q - 1 else []
                self.tr(ptq[:, j * 128:(j + 1) * 128],
                        self.cnb[:, j * 128:(j + 1) * 128], waits=w, incs=inc)
            t_trq = self.sPE.now()
            self.t_cnb_free = t_trq
            t_evq = self.sVE.inc()
            self.vcopy(self.xnT[:, 0:768], ptq[:, 0:768], waits=[t_trq], incs=[t_evq])
            self.pt2_free[0] = t_evq
            self.pt2_free[1] = t_evq
            ev_q = []
            for nchk in range(4):
                pq = self.PS[2 + (nchk % 2)]
                for rt in range(RT_Q):
                    w = mx([t_evq, t_wqb_f] +
                           ([ev_q[nchk - 2]] if (rt == 0 and nchk >= 2) else []) +
                           ([self.t_pq_free[nchk % 2]] if (rt == 0 and nchk < 2) else []))
                    inc = [self.sPE.inc()] if rt == RT_Q - 1 else []
                    self.mm(pq[:, 0:512], self.xnT[:, rt * 128:(rt + 1) * 128],
                            self.wqb[:, rt * D + nchk * 512:rt * D + (nchk + 1) * 512],
                            start=(rt == 0), stop=(rt == RT_Q - 1), waits=w, incs=inc)
                t_q = self.sPE.now()
                tev = self.sSE.inc()
                self.act(self.qf32[:, nchk * 512:(nchk + 1) * 512], pq[:, 0:512], AF.Copy,
                         waits=mx([t_q]), incs=[tev])
                ev_q.append(tev)
            t_qf = ev_q[-1]
            self.t_xnT_free = self.sPE.now()
            self.t_pq_free = [ev_q[2], ev_q[3]]
            t_c16 = self.dma("sync", self.c16s, self.dram["c16"][(base // 128) + tt, :, :],
                             self.dS, waits=mx([self.t_c16_free]))
            t_s16 = self.dma("sync", self.s16s, self.dram["s16"][(base // 128) + tt, :, :],
                             self.dS)
            t_qb = self.sSE.inc()
            self.scopy(self.xnb[:, :], self.qf32[:, :], waits=mx([t_qf, self.t_xnb_free]),
                       incs=[t_qb])

            def pe_ap(tn, off):
                return tn.rearrange("p (h c) -> p h c", h=H)[:, :, 64 + off:96 + off]
            x1, x2 = pe_ap(self.qf32[:, :], 0), pe_ap(self.qf32[:, :], 32)
            c16 = self.c16s.rearrange("p (h c) -> p h c", h=H)
            s16 = self.s16s.rearrange("p (h c) -> p h c", h=H)
            rtm = self.ropetmp.rearrange("p (h c) -> p h c", h=H)
            trs = self.x1st[:, 0:512].rearrange("p (h c) -> p h c", h=H)
            o1, o2 = pe_ap(self.xnb[:, :], 0), pe_ap(self.xnb[:, :], 32)
            f1 = self.sVE.inc()
            self.vtt(rtm, x1, c16, ALU.mult,
                     waits=mx([t_qf, t_c16, t_s16, self.t_ropetmp_free, self.t_x1st_free]))
            self.vtt(trs, x2, s16, ALU.mult, incs=[f1])
            f2 = self.sVE.inc()
            self.vtt(o1, rtm, trs, ALU.subtract, waits=[t_qb, f1], incs=[f2])
            self.vtt(rtm, x1, s16, ALU.mult, waits=[f2])
            f3 = self.sVE.inc()
            self.vtt(trs, x2, c16, ALU.mult, incs=[f3])
            tv_rope = self.sVE.inc()
            self.vtt(o2, rtm, trs, ALU.add, waits=[f3], incs=[tv_rope])
            self.t_c16_free = tv_rope
            self.t_qf32_free = tv_rope
            self.t_ropetmp_free = tv_rope
            self.t_x1st_free = tv_rope
            for g in range(4):
                pt = [self.PS45b[:, 0:1024], self.PS45b[:, 1024:2048]][g % 2]
                prev_ev = self.pt_free[g % 2]
                for j in range(4):
                    hh = g * 4 + j
                    w = mx(([tv_rope, t_qb] + bar if hh == 0 else []) +
                           ([prev_ev] if j == 0 else []))
                    inc = [self.sPE.inc()] if j == 3 else []
                    self.tr(pt[:, j * 128:(j + 1) * 128], self.xnb[:, hh * 128:(hh + 1) * 128],
                            waits=w, incs=inc)
                t_tr = self.sPE.now()
                t_ev = self.sVE.inc()
                dst = self.qcatT.rearrange("p (h t) -> p h t", h=H)[:, g * 4:(g + 1) * 4,
                                                                   tt * 128:(tt + 1) * 128]
                self.vcopy(dst, pt[:, 0:512].rearrange("p (h t) -> p h t", h=4),
                           waits=mx([t_tr] + (bar if (tt == 0 and g == 0) else [])),
                           incs=[t_ev])
                self.pt_free[g % 2] = t_ev
            self.t_xnb_free = self.sPE.now()
        t_qcat = self.sVE.now()
        self.tap("tap_qcatT%d" % ch, self.qcatT, [t_qcat])

        bar2 = self.barrier()
        t_wkvb = self.dma("gpsimd", self.wkvb, d["wkvb_p"], self.dG, waits=bar2)
        t_wkvb_f = None
        for rt in range(RT_KV):
            t_wkvb_f = self.sVE.inc()
            self.vts(self.wkvb[:, rt * 3072:(rt + 1) * 3072],
                     self.wkvb[:, rt * 3072:(rt + 1) * 3072],
                     self.nrmkv[:, rt:rt + 1], None, ALU.mult, ALU.bypass,
                     waits=[t_wkvb, self.t_nkv], incs=[t_wkvb_f])
        t_kpe_ld = self.dma("sync", self.kpeT, self.kpeTs, self.dS,
                            waits=mx(bar2 + [self.t_kpe_spill]))
        self.t_ps67_free = [self.pt2_free[0], self.pt2_free[1]]
        t_regt = mx(bar2 + [self.xst_free[0], self.xst_free[1]])
        t_out_toks = []
        st = {}

        def a_units(h):
            u = [("kn", kc) for kc in range(8)] + [("kpe",)] + \
                [("v", g) for g in range(8)]
            return u

        def emit_A(h, unit):
            p = h % 2
            s = st[h]
            kind = unit[0]
            if kind == "kn":
                kc = unit[1]
                half = s["half"] = 1 - s.get("half", 0)
                pk = self.PS67[0:64, half * 512:(half + 1) * 512]
                for rt in range(RT_KV):
                    w = mx([self.t_cnT_done, t_wkvb_f, self.t_ps67_free[half]]
                           + (bar2 if kc == 0 else []))
                    inc = [self.sPE.inc()] if rt == RT_KV - 1 else []
                    self.mm(pk, self.wkvb[:, rt * 3072 + h * 192:rt * 3072 + h * 192 + 64],
                            self.cnT[:, rt * T + kc * 512:rt * T + (kc + 1) * 512],
                            start=(rt == 0), stop=(rt == RT_KV - 1), waits=w, incs=inc)
                t_pk = self.sPE.now()
                tev = self.sSE.inc()
                self.scopy(self.ksT2[p][0:64, kc * 512:(kc + 1) * 512], pk,
                           waits=mx([t_pk] + t_regt), incs=[tev])
                s["ks_ev"].append(tev)
                self.t_ps67_free[half] = tev
            elif kind == "kpe":
                tev = self.sVE.inc()
                self.vcopy(self.ksT2[p][64:128, :], self.kpeT[0:64, :],
                           waits=mx([t_kpe_ld, s["ks_ev"][-1]] + t_regt), incs=[tev])
                s["kpe"] = tev
            elif kind == "v":
                g = unit[1]
                half = s["half"] = 1 - s.get("half", 0)
                for j in range(4):
                    kt = g * 4 + j
                    pv = self.PS67[:, half * 512 + j * 128:half * 512 + (j + 1) * 128]
                    for rt in range(RT_KV):
                        w = mx([self.t_ps67_free[half]] if (j == 0 and rt == 0) else [])
                        inc = [self.sPE.inc()] if (rt == RT_KV - 1 and j == 3) else []
                        self.mm(pv, self.cnT[:, rt * T + kt * 128:rt * T + (kt + 1) * 128],
                                self.wkvb[:, rt * 3072 + h * 192 + 64:rt * 3072 + (h + 1) * 192],
                                start=(rt == 0), stop=(rt == RT_KV - 1), waits=w, incs=inc)
                t_pv = self.sPE.now()
                tev = self.sSE.inc()
                self.scopy(self.vnat2[p][:, g * 512:(g + 1) * 512],
                           self.PS67[:, half * 512:(half + 1) * 512],
                           waits=mx([t_pv] + t_regt), incs=[tev])
                s["v_ev"].append(tev)
                self.t_ps67_free[half] = tev
        def pT4(qh, kt):
            return self.xn2T[:, qh * 4096 + (kt % 8) * 512:qh * 4096 + (kt % 8 + 1) * 512]

        def emit_C(h, weave):
            p = h % 2
            s = st[h]
            wq = 0
            accb = self.cnb[:, 0:512]
            po = [self.PS45[:, 0:512], self.PS45[:, 512:1024]]
            t_acc0 = []
            for qh in range(2):
                tok = self.sVE.inc()
                self.vmemset(self.accum2[qh], 0.0,
                             waits=mx([self.t_acc_free[qh]] + t_regt), incs=[tok])
                t_acc0.append(tok)
            t_exp, t_acc, t_pvk = {}, {}, {}
            t_accq = [None, None]
            nw = len(weave)
            for kt in range(KT_B):
                kc = kt // 4
                for qh in range(2):
                    ps2 = self.PS[(kt % 2) * 2 + qh]
                    tokS = self.sPE.inc()
                    self.mm(ps2[:, 0:512], self.ksT2[p][:, kt * 128:(kt + 1) * 128],
                            self.qcatT[:, h * CH + qh * 512:h * CH + (qh + 1) * 512],
                            start=True, stop=True,
                            waits=mx(([s["ks_ev"][kc], s["kpe"]] if qh == 0 else []) +
                                     [self.t_sping_free[(kt % 2) * 2 + qh]] +
                                     ([self.t_po_free[qh]] if kt == 0 else [])),
                            incs=[tokS])
                    t_exp[(kt, qh, "S")] = tokS
                if kt >= 2:
                    km = kt - 2
                    for qh in range(2):
                        tokP = self.sPE.inc()
                        self.mm(po[qh][:, 0:512], self.vnat2[p][:, km * 128:(km + 1) * 128],
                                pT4(qh, km), start=(km == 0), stop=False,
                                waits=mx([t_exp[(km, qh)], s["v_ev"][km // 4]] +
                                         ([self.t_po_free[qh]] if km == 0 else [])),
                                incs=[tokP])
                        t_pvk[(km, qh)] = tokP
                for qh in range(2):
                    ps2 = self.PS[(kt % 2) * 2 + qh]
                    tokE = self.sSE.inc()
                    self.act(pT4(qh, kt), ps2[:, 0:512], AF.Exp, scale=SCALE,
                             waits=mx([t_exp[(kt, qh, "S")], t_pvk.get((kt - 8, qh)),
                                       t_acc.get((kt - 8, qh))] +
                                      (t_regt if kt == 0 else [])),
                             incs=[tokE])
                    t_exp[(kt, qh)] = tokE
                    self.t_sping_free[(kt % 2) * 2 + qh] = tokE
                    tokA = self.sVE.inc()
                    self.vtt(self.accum2[qh], self.accum2[qh], pT4(qh, kt), ALU.add,
                             waits=mx([tokE] + ([t_acc0[qh]] if kt == 0 else [])), incs=[tokA])
                    t_acc[(kt, qh)] = tokA
                    t_accq[qh] = tokA
                hi = (nw * (kt + 1)) // KT_B
                while wq < hi:
                    emit_A(h + 1, weave[wq])
                    wq += 1
            t_pv_done = []
            for km in (KT_B - 2, KT_B - 1):
                for qh in range(2):
                    tokP = self.sPE.inc()
                    self.mm(po[qh][:, 0:512], self.vnat2[p][:, km * 128:(km + 1) * 128],
                            pT4(qh, km), start=False, stop=(km == KT_B - 1),
                            waits=mx([t_exp[(km, qh)], s["v_ev"][7]]), incs=[tokP])
                    if km == KT_B - 1:
                        t_pv_done.append(tokP)
            return (h, t_pv_done, t_exp, t_accq)

        def emit_C_denoms(state):
            h, t_pv_done, t_exp, t_accq = state
            p = h % 2
            accb = self.cnb[:, 0:512]
            po = [self.PS45[:, 0:512], self.PS45[:, 512:1024]]
            for qh in range(2):
                t_accb = self.sSE.inc()
                self.scopy(accb, self.accum2[qh],
                           waits=mx([t_accq[qh], self.t_accb_free,
                                     self.t_cnb_free]), incs=[t_accb])
                self.t_acc_free[qh] = t_accb
                pl = self.PS[qh]
                tokL = self.sPE.inc()
                self.mm(pl[0:1, 0:512], self.onescol[:, 0:1], accb,
                        start=True, stop=True,
                        waits=mx([t_accb, self.t_ones, t_exp[(KT_B - 1, qh)],
                                  t_exp[(KT_B - 2, qh)]]),
                        incs=[tokL])
                self.t_accb_free = tokL
                self.t_cnb_free = tokL
                tv_l = self.sVE.inc()
                self.vcopy(self.lrow[0:1, 0:512], pl[0:1, 0:512], waits=[tokL], incs=[tv_l])
                ts_l1 = self.sSE.inc()
                self.act(self.rrow[0:1, 0:512], self.lrow[0:1, 0:512], AF.Ln,
                         waits=[tv_l], incs=[ts_l1])
                ts_l2 = self.sSE.inc()
                self.act(self.rrowb[0:1, 0:512], self.rrow[0:1, 0:512], AF.Exp, scale=-1.0,
                         waits=[ts_l1], incs=[ts_l2])
                tokB = self.sPE.inc()
                self.mm(pl[:, 0:512], self.onesrow[0:1, :], self.rrowb[0:1, 0:512],
                        start=True, stop=True, waits=[ts_l2], incs=[tokB])
                rbc = self.accum2[qh]
                tv_rb = self.sVE.inc()
                self.vcopy(rbc, pl[:, 0:512], waits=[tokB, t_accb], incs=[tv_rb])
                tv_out = self.sVE.inc()
                self.vtt(self.qcatT[:, h * CH + qh * 512:h * CH + (qh + 1) * 512],
                         po[qh][:, 0:512], rbc, ALU.mult,
                         waits=[t_pv_done[qh], tv_rb], incs=[tv_out])
                t_out_toks.append(tv_out)
                self.t_po_free[qh] = tv_out
                self.t_acc_free[qh] = tv_out
                self.t_sping_free[qh] = tv_rb

        st[0] = {"ks_ev": [], "v_ev": []}
        for u in a_units(0):
            emit_A(0, u)
        for h in range(1, H):
            st[h] = {"ks_ev": [], "v_ev": []}
            stc = emit_C(h - 1, a_units(h))
            emit_C_denoms(stc)
        stc = emit_C(H - 1, [])
        emit_C_denoms(stc)
        self.tap("tap_attnT%d" % ch, self.qcatT, [self.sVE.now()])
        return t_out_toks

    def p2b(self, ch, t_out_toks):
        base = ch * CH
        bar = self.barrier()
        self.t_xnb_free2 = [bar[0], bar[0]]
        if t_out_toks:
            self.pt_free = [t_out_toks[-1], t_out_toks[-1]]
        fin4 = self.fin4
        t_slab = {}
        first_dma = [True]

        def issue_wo(tg, fq, g):
            seq = fq * 2 + g
            buf = self.woslab[seq % 2]
            w = mx([self.t_woslab_free[seq % 2]] + (bar if first_dma[0] else []))
            first_dma[0] = False
            t_slab[(tg, fq, g)] = self.dma("gpsimd", buf[:, :],
                                           self.dram["wo_p"][fq, g, :, :], self.dG,
                                           waits=w)

        def wo_group(tg, jobs):
            issue_wo(tg, 0, 0)
            issue_wo(tg, 0, 1)
            evs = [list(self.t_psbank)]
            for fq in range(4):
                bank_tok = [None] * 4
                for g in range(2):
                    seq = fq * 2 + g
                    buf = self.woslab[seq % 2]
                    for tt4 in range(4):
                        tt = tg * 4 + tt4
                        for j in range(8):
                            hv = g * 8 + j
                            w = mx(([t_slab[(tg, fq, g)]] if (tt4 == 0 and j == 0) else []) +
                                   ([evs[fq][tt4]] if (g == 0 and j == 0) else []) +
                                   (mx(t_out_toks) + bar
                                    if (tg == 0 and fq == 0 and g == 0 and tt4 == 0
                                        and j == 0) else []))
                            inc = ([self.sPE.inc()]
                                   if (j == 7 and (g == 1 or tt4 == 3)) else [])
                            self.mm(self.PS[tt4][:, 0:512],
                                    self.qcatT[:, hv * CH + tt * 128:hv * CH + (tt + 1) * 128],
                                    buf[:, j * 512:(j + 1) * 512],
                                    start=(g == 0 and j == 0), stop=(g == 1 and j == 7),
                                    waits=w, incs=inc)
                            if g == 1 and j == 7:
                                bank_tok[tt4] = self.sPE.now()
                    self.t_woslab_free[seq % 2] = self.sPE.now()
                    if seq + 2 < 8:
                        issue_wo(tg, fq + 1, g)
                ev_fq = []
                for tt4 in range(4):
                    tok = self.sSE.inc()
                    fguard = self.t_finbuf4[tt4] if fq == 0 else None
                    fguard = list(fguard) if isinstance(fguard, tuple) else mx([fguard])
                    self.act(fin4[tt4][:, fq * 512:(fq + 1) * 512], self.PS[tt4][:, 0:512],
                             AF.Copy,
                             waits=mx([bank_tok[tt4]] + fguard +
                                      (bar if (tg == 0 and fq == 0) else [])),
                             incs=[tok])
                    ev_fq.append(tok)
                evs.append(ev_fq)
                if jobs:
                    jobs.pop(0)()
            self.t_psbank = evs[-1]
            return evs[-1]

        def ln_tile_a(tt, t_fin_ev):
            par = tt % 2
            fb = fin4[tt % 4]
            scr2 = [self.qf32, self.x1st][par]
            g2 = [self.t_qf32_free, self.t_x1st_free][par]
            rstd, bias, ts, tv = self.ln_stats([(fb, D)], D, EPS_LN, 32 * par,
                                               mx([t_fin_ev, g2]), scratch=scr2)
            t_ln = self.sSE.inc()
            self.act(scr2[:, :], fb, AF.Identity, bias=bias, scale=rstd,
                     waits=mx([ts, tv]), incs=[t_ln])
            self.stats_guard[32 * par] = t_ln
            row0 = base + tt * 128
            t_x = self.dma("sync", self.xst[par][:, :],
                           self.dram["xmy"][row0:row0 + 128, :], self.dS,
                           waits=mx([self.xst_free[par]] + self.t_regt_heads))
            tv_x1 = self.sVE.inc()
            self.vtt(fb, scr2[:, :], self.xst[par][:, :], ALU.add,
                     waits=mx([t_ln, t_x]), incs=[tv_x1])
            self.xst_free[par] = tv_x1
            if par == 0:
                self.t_qf32_free = tv_x1
            else:
                self.t_x1st_free = tv_x1
            t_x1w = self.dma("sync", self.x1s[row0:row0 + 128, :], fb, self.dO,
                             waits=[tv_x1])
            self.t_x1w_last = t_x1w
            rstd2, bias2, ts2, tv2 = self.ln_stats([(fb, D)], D, EPS_LN, 16 + 32 * par,
                                                   [tv_x1], scratch=scr2)
            xnb2 = [self.xnb, self.xnT][par]
            t_xn2 = self.sSE.inc()
            self.act(xnb2[:, :], fb, AF.Identity, bias=bias2, scale=rstd2,
                     waits=mx([ts2, tv2, self.t_xnb_free2[par]]), incs=[t_xn2])
            self.stats_guard[16 + 32 * par] = t_xn2
            self.t_finbuf4[tt % 4] = (t_xn2, t_x1w)
            return t_xn2

        def ln_tile_b(tt, t_xn2):
            par = tt % 2
            xnb2 = [self.xnb, self.xnT][par]
            for g in range(4):
                pt = [self.PS45b[:, 0:1024], self.PS45b[:, 1024:2048]][g % 2]
                prev_ev = self.pt_free[g % 2]
                for j in range(4):
                    dt = g * 4 + j
                    w = mx(([t_xn2, self.t_xn2T_free] if dt == 0 else []) +
                           ([prev_ev] if j == 0 else []))
                    inc = [self.sPE.inc()] if j == 3 else []
                    self.tr(pt[:, j * 128:(j + 1) * 128], xnb2[:, dt * 128:(dt + 1) * 128],
                            waits=w, incs=inc)
                t_tr = self.sPE.now()
                t_ev = self.sVE.inc()
                dst = self.xn2T[:, :].rearrange("p (d t) -> p d t", d=DT)[:, g * 4:(g + 1) * 4,
                                                                         tt * 128:(tt + 1) * 128]
                self.vcopy(dst, pt[:, 0:512].rearrange("p (d t) -> p d t", d=4),
                           waits=mx([t_tr, self.t_xn2T_free]), incs=[t_ev])
                self.pt_free[g % 2] = t_ev
            self.t_xnb_free2[par] = self.sPE.now()

        evs0 = wo_group(0, [])
        xn2_toks = {}

        def mk_a(tt, fev):
            def f():
                xn2_toks[tt] = ln_tile_a(tt, fev)
            return f

        def mk_b(tt):
            def f():
                ln_tile_b(tt, xn2_toks[tt])
            return f
        mk_a(0, evs0[0])()
        mk_a(1, evs0[1])()
        mk_b(0)()
        mk_a(2, evs0[2])()
        mk_b(1)()
        mk_a(3, evs0[3])()
        jobs = [mk_b(2), mk_b(3)]
        evs1 = wo_group(1, jobs)
        while jobs:
            jobs.pop(0)()
        for tt4 in range(4):
            mk_a(4 + tt4, evs1[tt4])()
            if tt4 >= 1:
                mk_b(4 + tt4 - 1)()
        mk_b(7)()
        t_xn2T = self.sVE.now()
        self.tap("tap_xn2T%d" % ch, self.xn2T[:, :], [t_xn2T])
        return t_xn2T

    def p3(self, ch, t_xn2T):
        base = ch * CH
        NT = CH // 128
        bar = self.barrier()
        pg = self.PS67[:, 0:512]
        st = self.stats
        for tt in range(NT):
            for dt in range(DT):
                w = mx([t_xn2T, self.t_gw] +
                       ([self.t_pg_free, self.t_ps67_free[0], self.t_ps67_free[1]]
                        if dt == 0 else []))
                inc = [self.sPE.inc()] if dt == DT - 1 else []
                self.mm(pg[:, 0:8], self.xn2T[:, dt * CH + tt * 128:dt * CH + (tt + 1) * 128],
                        self.gw[:, dt * E:(dt + 1) * E], start=(dt == 0), stop=(dt == DT - 1),
                        waits=w, incs=inc)
            t_g = self.sPE.now()
            tv1 = self.sVE.inc()
            self.vred(st[:, 64:65], pg[:, 0:8], ALU.max,
                      waits=mx([t_g, self.stats_guard.get(64)]), incs=[tv1])
            tv1b = self.sVE.inc()
            self.vts(st[:, 65:66], st[:, 64:65], -1.0, None, ALU.mult, ALU.bypass,
                     waits=[tv1], incs=[tv1b])
            ts_e = self.sSE.inc()
            self.act(self.e8[:, 0:8], pg[:, 0:8], AF.Exp, bias=st[:, 65:66],
                     waits=mx([tv1b, self.t_e8_free]), incs=[ts_e])
            self.t_pg_free = ts_e
            tv1c = self.sVE.inc()
            self.vred(st[:, 66:67], self.e8[:, 0:8], ALU.add, waits=[ts_e], incs=[tv1c])
            tv2 = self.sVE.inc()
            self.vrecip(st[:, 67:68], st[:, 66:67], waits=[tv1c], incs=[tv2])
            tv3 = self.sVE.inc()
            self.vts(self.combtmp[:, 0:8], self.e8[:, 0:8], st[:, 67:68], None, ALU.mult,
                     ALU.bypass, waits=[tv2], incs=[tv3])
            self.t_e8_free = tv3
            self.stats_guard[64] = tv3
            tv4 = self.sVE.inc()
            self.vred(st[:, 68:69], self.combtmp[:, 0:8], ALU.max, waits=[tv3], incs=[tv4])
            tv5 = self.sVE.inc()
            self.vts(self.tmp8[:, 0:8], self.combtmp[:, 0:8], st[:, 68:69], -1e30,
                     ALU.is_ge, ALU.mult, waits=mx([tv4]), incs=[tv5])
            tv6 = self.sVE.inc()
            self.vtt(self.tmp8[:, 0:8], self.combtmp[:, 0:8], self.tmp8[:, 0:8], ALU.add,
                     waits=[tv5], incs=[tv6])
            tv7 = self.sVE.inc()
            self.vred(st[:, 69:70], self.tmp8[:, 0:8], ALU.max, waits=[tv6], incs=[tv7])
            tv8 = self.sVE.inc()
            self.vts(self.tmp8[:, 0:8], self.combtmp[:, 0:8], st[:, 69:70], None,
                     ALU.is_ge, ALU.bypass, waits=[tv7], incs=[tv8])
            tv9 = self.sVE.inc()
            self.vtt(self.comb[:, tt * E:(tt + 1) * E], self.combtmp[:, 0:8],
                     self.tmp8[:, 0:8], ALU.mult, waits=[tv8], incs=[tv9])
        t_comb = self.sVE.now()
        self.tap("tap_comb%d" % ch, self.comb[:, :], [t_comb])

        t_y0 = self.sVE.inc()
        self.vmemset(self.Y, 0.0, waits=mx([self.t_Y_free, self.t_cnT_spill] + bar),
                     incs=[t_y0])
        t_Y_tile = {}

        specs = [("wgu_p", "wd_p", e, True) for e in range(E)] + \
                [("swgu_p", "swd_p", s, False) for s in range(2)]
        gu_tok, wd_tok = {}, {}

        def issue_gu(si, ht):
            gname, _dn, ei, _r = specs[si]
            buf = self.wgu[ht % 3]
            fin_g = []
            if si == 0 and ht == 2:
                for tpl in (self.t_finbuf4[2], self.t_finbuf4[3]):
                    fin_g += list(tpl) if isinstance(tpl, tuple) else [tpl]
            w = mx([self.t_wgu_free[ht % 3]] + (bar if (si == 0 and ht < 3) else []) + fin_g)
            tg = self.dma("gpsimd", buf.rearrange("p (s f) -> p s f", s=2),
                          self.dram[gname][ei, ht, :, :, :].rearrange("s p f -> p s f"),
                          self.dG, waits=w)
            gu_tok[(si, ht)] = tg

        def issue_wd(si, fc):
            _gn, dname, ei, _r = specs[si]
            buf = self.wds[fc % 2]
            wd_tok[(si, fc)] = self.dma(
                "gpsimd", buf.rearrange("p (h f) -> p h f", h=HT),
                self.dram[dname][ei, fc, :, :].rearrange("(h p) f -> p h f", p=128),
                self.dG, waits=mx([self.t_wds_free[fc % 2]] +
                                  (bar if (si == 0 and fc < 2) else [])))

        issue_gu(0, 0)
        issue_gu(0, 1)
        issue_gu(0, 2)
        for si, (gname, dname, ei, routed) in enumerate(specs):
            t_gu = {ht: (gu_tok[(si, ht)], gu_tok[(si, ht)]) for ht in (0, 1, 2)}
            for ht in range(HT):
                bg = self.wgu[ht % 3][:, 0:2048]
                bu = self.wgu[ht % 3][:, 2048:4096]
                pgt = [self.PS[0], self.PS[1]]
                pup = [self.PS[2], self.PS[3]]
                for nh in range(2):
                    for dt in range(DT):
                        w = mx([t_gu[ht][0], t_xn2T] +
                               ([self.t_psg_free, self.t_psu_free]
                                if (dt == 0 and nh == 0) else []))
                        self.mm(pgt[nh][:, 0:512], bg[:, dt * 128:(dt + 1) * 128],
                                self.xn2T[:, dt * CH + nh * 512:dt * CH + (nh + 1) * 512],
                                start=(dt == 0), stop=(dt == DT - 1), waits=w)
                        w2 = mx([t_gu[ht][1]] if (dt == 0 and nh == 0) else [])
                        inc = [self.sPE.inc()] if (dt == DT - 1 and nh == 1) else []
                        self.mm(pup[nh][:, 0:512], bu[:, dt * 128:(dt + 1) * 128],
                                self.xn2T[:, dt * CH + nh * 512:dt * CH + (nh + 1) * 512],
                                start=(dt == 0), stop=(dt == DT - 1), waits=w2, incs=inc)
                t_G = self.sPE.now()
                self.t_wgu_free[ht % 3] = t_G
                if ht + 3 < HT:
                    issue_gu(si, ht + 3)
                    t_gu[ht + 3] = (gu_tok[(si, ht + 3)], gu_tok[(si, ht + 3)])
                elif ht == HT - 1:
                    issue_wd(si, 0)
                    issue_wd(si, 1)
                ts_a = self.sSE.inc()
                self.act(self.sg[:, 0:512], pgt[0][:, 0:512], AF.Silu,
                         waits=mx([t_G, self.t_sg_free] + bar), incs=[ts_a])
                ts_b = self.sSE.inc()
                self.act(self.sg[:, 512:1024], pgt[1][:, 0:512], AF.Silu, incs=[ts_b])
                tva = self.sVE.inc()
                self.vstt(self.Gs[:, ht * CH:ht * CH + 512], pup[0][:, 0:512], 1.0,
                          self.sg[:, 0:512], ALU.mult, ALU.mult, waits=[ts_a], incs=[tva])
                tvb = self.sVE.inc()
                self.vstt(self.Gs[:, ht * CH + 512:(ht + 1) * CH], pup[1][:, 0:512], 1.0,
                          self.sg[:, 512:1024], ALU.mult, ALU.mult, waits=[ts_b], incs=[tvb])
                self.t_psg_free = ts_b
                self.t_psu_free = tvb
                self.t_sg_free = tvb
            t_Gs = self.sVE.now()
            t_wd = {fc: wd_tok[(si, fc)] for fc in (0, 1)}
            for fc in range(4):
                buf = self.wds[fc % 2]
                for tt in range(NT):
                    pdx = [self.PS45[:, 0:512], self.PS45[:, 512:1024]][tt % 2]
                    for ht in range(HT):
                        w = mx([t_wd[fc], t_Gs] + ([self.t_pd_free[tt % 2]] if ht == 0 else []))
                        inc = [self.sPE.inc()] if ht == HT - 1 else []
                        self.mm(pdx[:, 0:512],
                                self.Gs[:, ht * CH + tt * 128:ht * CH + (tt + 1) * 128],
                                buf[:, ht * 512:(ht + 1) * 512],
                                start=(ht == 0), stop=(ht == HT - 1), waits=w, incs=inc)
                    t_d = self.sPE.now()
                    yap = self.Y[:, tt * D + fc * 512:tt * D + (fc + 1) * 512]
                    tv_y = self.sVE.inc()
                    if routed:
                        self.vstt(yap, pdx[:, 0:512], self.comb[:, tt * E + ei:tt * E + ei + 1],
                                  yap, ALU.mult, ALU.add, waits=mx([t_d, t_comb, t_y0]),
                                  incs=[tv_y])
                    else:
                        self.vtt(yap, yap, pdx[:, 0:512], ALU.add, waits=mx([t_d, t_y0]),
                                 incs=[tv_y])
                    self.t_pd_free[tt % 2] = tv_y
                    t_Y_tile[tt] = tv_y
                self.t_wds_free[fc % 2] = self.sPE.now()
                if fc + 2 < 4:
                    issue_wd(si, fc + 2)
                    t_wd[fc + 2] = wd_tok[(si, fc + 2)]
                if fc == 1 and si + 1 < len(specs):
                    issue_gu(si + 1, 0)
                    issue_gu(si + 1, 1)
                    issue_gu(si + 1, 2)
        t_Y = self.sVE.now()
        self.tap("tap_Y%d" % ch, self.Y, [t_Y])

        xbuf = [self.x1st, self.qf32]
        scpar = [self.qf32, self.x1st]
        t_xb_free = [self.t_x1st_free, self.t_qf32_free]
        for tt in range(NT):
            par = tt % 2
            yap = self.Y[:, tt * D:(tt + 1) * D]
            rstd, bias, ts, tv = self.ln_stats([(yap, D)], D, EPS_LN, 32 * par,
                                               mx([t_Y_tile[tt], t_xb_free[1 - par]]),
                                               scratch=scpar[par])
            t_ln = self.sSE.inc()
            self.act(yap, yap, AF.Identity, bias=bias, scale=rstd,
                     waits=mx([ts, tv]), incs=[t_ln])
            self.stats_guard[32 * par] = t_ln
            t_x1 = self.dma("sync", xbuf[par][:, :],
                            self.x1s[base + tt * 128:base + (tt + 1) * 128, :], self.dS,
                            waits=mx([t_xb_free[par], self.t_x1w_last]))
            tv_o = self.sVE.inc()
            self.vtt(xbuf[par][:, :], xbuf[par][:, :], yap, ALU.add,
                     waits=mx([t_ln, t_x1]), incs=[tv_o])
            t_st = self.dma("sync", self.out[base + tt * 128:base + (tt + 1) * 128, :],
                            xbuf[par][:, :], self.dO, waits=[tv_o])
            t_xb_free[par] = t_st
        self.t_x1st_free = t_xb_free[0]
        self.t_qf32_free = t_xb_free[1]
        self.t_Y_free = self.sVE.now()
        self.t_xn2T_free = self.sPE.now()

    def build(self):
        self.stats_guard = {}
        self.xst_free = [None, None]
        self.pt_free = [None, None]
        for name in ("t_pc_free t_c16_free "
                     "t_x1st_free t_pg_free t_Y_free t_psg_free "
                     "t_psu_free t_sg_free t_cnb_free t_xnb_free t_qf32_free "
                     "t_ropetmp_free t_x1w_last "
                     "t_xn2T_free t_e8_free t_cnT_spill t_kpe_spill t_xnT_free").split():
            setattr(self, name, None)
        self.pt2_free = [None, None]
        self.t_cnb_free2 = [None, None]
        self.t_rms_scr = None
        self.t_ps67_free = [None, None]
        self.t_sping_free = [None, None, None, None]
        self.t_po_free = [None, None]
        self.t_acc_free = [None, None]
        self.t_accb_free = None
        self.t_cnb_free = None
        self.t_pq_free = [None, None]
        self.t_woslab_free = [None, None]
        self.t_finbuf4 = [None, None, None, None]
        self.t_psbank = [None, None, None, None]
        self.t_wgu_free = [None, None, None]
        self.t_wds_free = [None, None]
        self.t_pd_free = [None, None]
        self.t_regt_heads = []

        self.p0()
        self.p1()
        if _PHASES != "P1":
            for ch in range(NCH):
                bar = self.barrier()
                t_out = self.p2(ch, bar)
                if _PHASES == "P2" and ch == 0:
                    break
                self.t_regt_heads = self.barrier()
                t_xn2T = self.p2b(ch, t_out)
                if _PHASES == "P2B" and ch == 0:
                    break
                self.p3(ch, t_xn2T)
        self.P.emit("sync", lambda e: e.nop(),
                    waits=[self.dO.now(), self.dS.now(), self.dG.now()])
        block = self.es.enter_context(self.nc.Block())
        self.P.install(block)
        self.es.close()
        return self.nc


def _prep_weights(inp):
    f = np.ascontiguousarray
    w = {}
    w["wqa_p"] = f(inp["L0_attn_wq_a"].reshape(DT, 128, QR).transpose(1, 0, 2).reshape(128, DT * QR))
    w["wqb_p"] = f(inp["L0_attn_wq_b"].reshape(RT_Q, 128, D).transpose(1, 0, 2).reshape(128, RT_Q * D))
    w["qnorm_p"] = f(inp["L0_attn_q_a_norm_w"].reshape(RT_Q, 128).T)
    w["wkva_p"] = f(inp["L0_attn_wkv_a"].reshape(DT, 128, 576).transpose(1, 0, 2).reshape(128, DT * 576))
    w["wkvb_p"] = f(inp["L0_attn_wkv_b"].reshape(RT_KV, 128, 3072).transpose(1, 0, 2).reshape(128, RT_KV * 3072))
    w["kvnorm_p"] = f(inp["L0_attn_kv_a_norm_w"].reshape(RT_KV, 128).T)
    w["wo_p"] = f(inp["L0_attn_wo"].reshape(2, 8, 128, 4, 512)
                  .transpose(3, 0, 2, 1, 4).reshape(4, 2, 128, 4096))
    w["gw_p"] = f(inp["L0_ffn_gate_w"].reshape(DT, 128, E).transpose(1, 0, 2).reshape(128, DT * E))
    wg = inp["L0_ffn_w_gate"].reshape(E, DT, 128, HT, 128).transpose(0, 3, 2, 1, 4) \
        .reshape(E, HT, 128, DT * 128)
    wu = inp["L0_ffn_w_up"].reshape(E, DT, 128, HT, 128).transpose(0, 3, 2, 1, 4) \
        .reshape(E, HT, 128, DT * 128)
    w["wgu_p"] = f(np.stack([wg, wu], axis=2))
    w["wd_p"] = f(inp["L0_ffn_w_down"].reshape(E, HT, 128, 4, 512).transpose(0, 3, 1, 2, 4)
                  .reshape(E, 4, HT * 128, 512))
    swg = inp["L0_ffn_sw_gate"].reshape(DT, 128, 2, HT, 128) \
        .transpose(2, 3, 1, 0, 4).reshape(2, HT, 128, DT * 128)
    swu = inp["L0_ffn_sw_up"].reshape(DT, 128, 2, HT, 128) \
        .transpose(2, 3, 1, 0, 4).reshape(2, HT, 128, DT * 128)
    w["swgu_p"] = f(np.stack([swg, swu], axis=2))
    swd = inp["L0_ffn_sw_down"].reshape(2, HT, 128, 4, 512)
    w["swd_p"] = f(swd.transpose(0, 3, 1, 2, 4).reshape(2, 4, HT * 128, 512))
    return w


def _rope_tables():
    inv_freq = 1.0 / 10000.0 ** (np.arange(0, ROPE, 2, dtype=np.float32) / ROPE)
    t = np.arange(T, dtype=np.float32)
    freqs = np.outer(t, inv_freq)
    return np.cos(freqs).astype(np.float32), np.sin(freqs).astype(np.float32)


def build_in_maps(inputs):
    x = np.asarray(inputs["L0_attn_x"], dtype=np.float32)
    w = _prep_weights({k: np.asarray(v, dtype=np.float32) for k, v in inputs.items()
                       if k != "L0_attn_x"})
    cos, sin = _rope_tables()
    consts = dict(ident=np.eye(128, dtype=BF),
                  onesrow=np.ones((1, 128), dtype=BF),
                  onesneg=np.full((1, 128), -1.0, dtype=BF),
                  onescol=np.ones((128, 1), dtype=BF),
                  cosb=np.ascontiguousarray(cos.reshape(KT_B, 128, 32).transpose(1, 0, 2)
                                            .reshape(128, KT_B * 32)),
                  sinb=np.ascontiguousarray(sin.reshape(KT_B, 128, 32).transpose(1, 0, 2)
                                            .reshape(128, KT_B * 32)))
    in_maps = []
    for core in range(8):
        b, half = core // 2, core % 2
        my0 = half * MY
        c16 = np.ascontiguousarray(
            np.tile(cos[my0:my0 + MY].reshape(MY // 128, 128, 1, 32), (1, 1, 16, 1))
            .reshape(MY // 128, 128, 512))
        s16 = np.ascontiguousarray(
            np.tile(sin[my0:my0 + MY].reshape(MY // 128, 128, 1, 32), (1, 1, 16, 1))
            .reshape(MY // 128, 128, 512))
        m = dict(w)
        m.update(consts)
        m["xb"] = np.ascontiguousarray(x[b])
        m["xmy"] = np.ascontiguousarray(x[b, my0:my0 + MY])
        m["c16"] = c16
        m["s16"] = s16
        in_maps.append(m)
    return in_maps


def kernel(**inputs):
    in_maps = build_in_maps(inputs)
    bld = Builder(_TAPS)
    nc = bld.build()
    res = run_bass_kernel_spmd(nc, in_maps, core_ids=list(range(8)))
    out = np.empty((B, T, D), dtype=np.float32)
    for core in range(8):
        b, half = core // 2, core % 2
        out[b, half * MY:(half + 1) * MY, :] = res.results[core]["out"]
    kernel._last_results = res
    return out
